# version 25
# baseline (speedup 1.0000x reference)
import contextlib
import numpy as np
import ml_dtypes
import concourse.bass as bass
import concourse.mybir as mybir
from concourse.bass_utils import run_bass_kernel_spmd

F32 = mybir.dt.float32
BF16 = mybir.dt.bfloat16
I32 = mybir.dt.int32
AF = mybir.ActivationFunctionType
OP = mybir.AluOpType
AX = mybir.AxisListType

D = 1024
KT = 8
HALF = 512
NH = 8
TT = 4
FF = 4096
FT = 32
EPS5 = 1e-5
EPS8 = 1e-8

COMPUTE_ENGINES = ("tensor", "vector", "scalar", "gpsimd")
DMA_QUEUES = {"q_sync": "sync", "q_gpsimd": "gpsimd", "q_scalar": "scalar",
              "q_vector": "vector"}


class Buf:
    __slots__ = ("name", "last_write", "readers")

    def __init__(self, name):
        self.name = name
        self.last_write = None
        self.readers = []


class Sched:

    def __init__(self, nc, dma_group=None, total_groups=()):
        self.nc = nc
        self.items = []
        self._bufs = {}
        self.sems = None
        self.dma_group = dma_group or (lambda buf: buf)
        self.total_groups = set(total_groups)

    def buf(self, name):
        b = self._bufs.get(name)
        if b is None:
            b = self._bufs[name] = Buf(name)
        return b

    def emit(self, stream, fn, reads=(), writes=(), sreads=(), tag=""):
        idx = len(self.items)
        reads = [self.buf(r) if isinstance(r, str) else r for r in reads]
        sreads = [self.buf(r) if isinstance(r, str) else r for r in sreads]
        writes = [self.buf(w) if isinstance(w, str) else w for w in writes]
        rdeps = set()
        wdeps = set()
        hard = set()
        for b in reads:
            if b.last_write is not None:
                rdeps.add(b.last_write)
        for b in sreads:
            if b.last_write is not None:
                hard.add(b.last_write)
        for b in writes:
            if b.last_write is not None:
                wdeps.add(b.last_write)
            for r in b.readers:
                wdeps.add(r)
        if stream in DMA_QUEUES:
            assert writes, f"DMA item {tag} needs a write buf for its semaphore"
            skey = "d_" + self.dma_group(writes[0].name)
        else:
            skey = stream
        self.items.append(dict(idx=idx, stream=stream, fn=fn, rdeps=rdeps,
                               wdeps=wdeps, hard=hard, tag=tag, signals=False,
                               skey=skey, sem_val=None, waits=None))
        for b in list(reads) + list(sreads):
            b.readers.append(idx)
        for b in writes:
            b.last_write = idx
            b.readers = []
        return idx

    def build(self):
        items = self.items
        for it in items:
            if it["stream"] in DMA_QUEUES:
                it["signals"] = True
            for d in list(it["rdeps"]) + list(it["wdeps"]):
                dep = items[d]
                if dep["stream"] == "tensor" and it["stream"] == "tensor":
                    continue
                dep["signals"] = True
            for d in it["hard"]:
                items[d]["signals"] = True
        counters = {}
        for it in items:
            if it["signals"]:
                step = 16 if it["stream"] in DMA_QUEUES else 1
                counters[it["skey"]] = counters.get(it["skey"], 0) + step
                it["sem_val"] = counters[it["skey"]]
        self.sem_totals = dict(counters)
        for it in items:
            need = {}

            def add(dep, force_total):
                s, v = dep["skey"], dep["sem_val"]
                assert v is not None
                if force_total and s.startswith("d_") and s[2:] in self.total_groups:
                    v = self.sem_totals[s]
                if need.get(s, 0) < v:
                    need[s] = v
            for d in it["rdeps"]:
                dep = items[d]
                if dep["stream"] == "tensor" and it["stream"] == "tensor":
                    continue
                add(dep, True)
            for d in it["wdeps"]:
                dep = items[d]
                if dep["stream"] == "tensor" and it["stream"] == "tensor":
                    continue
                if (dep["skey"] == it["skey"] and it["stream"] in DMA_QUEUES
                        and it["skey"].startswith("d_")
                        and it["skey"][2:] in self.total_groups):
                    continue
                add(dep, False)
            for d in it["hard"]:
                add(items[d], True)
            it["waits"] = need
        seen = {}
        for it in items:
            ee = DMA_QUEUES.get(it["stream"], it["stream"])
            got = seen.setdefault(ee, {})
            w = {}
            for s, v in sorted(it["waits"].items()):
                if got.get(s, 0) < v:
                    w[s] = v
                    got[s] = v
            it["waits"] = w

    def materialize(self, block):
        items = self.items
        sems = self.sems

        def run_stream(eng_name, eng):
            for it in items:
                if DMA_QUEUES.get(it["stream"], it["stream"]) != eng_name:
                    continue
                for s, v in sorted(it["waits"].items()):
                    eng.wait_ge(sems[s], v)
                inst = it["fn"](eng)
                if it["signals"]:
                    step = 16 if it["stream"] in DMA_QUEUES else 1
                    assert inst is not None, f"item {it['idx']} {it['tag']}"
                    inst.then_inc(sems[it["skey"]], step)

        for name in list(COMPUTE_ENGINES) + ["sync"]:
            if any(DMA_QUEUES.get(it["stream"], it["stream"]) == name
                   for it in self.items):
                getattr(block, name)(lambda eng, n=name: run_stream(n, eng))


def build_program():
    nc = bass.Bass("TRN2", target_bir_lowering=False)
    dr = lambda n, s, d: nc.dram_tensor(n, s, d, kind="ExternalInput").ap()
    xcore = dr("xcore", [NH * HALF, D], F32)
    ident = dr("ident", [128, 128], BF16)
    Wk = dr("L0_attn_Wk", [D, D], F32)
    Wv = dr("L0_attn_Wv", [D, D], F32)
    Wr = dr("L0_attn_Wr", [D, D], F32)
    Wo = dr("L0_attn_Wo", [D, D], F32)
    Wfr = dr("L0_attn_Wfr", [D, D], F32)
    Wfk = dr("L0_attn_Wfk", [D, FF], F32)
    Wfv = dr("L0_attn_Wfv", [FF, D], F32)
    g1 = dr("L0_attn_ln1_g", [D], F32)
    b1 = dr("L0_attn_ln1_b", [D], F32)
    g2 = dr("L0_attn_ln2_g", [D], F32)
    b2 = dr("L0_attn_ln2_b", [D], F32)
    td = dr("L0_attn_time_decay", [D], F32)
    tf = dr("L0_attn_time_first", [D], F32)
    mk = dr("L0_attn_mix_k", [D], F32)
    mvv = dr("L0_attn_mix_v", [D], F32)
    mr = dr("L0_attn_mix_r", [D], F32)
    g3 = dr("L0_ffn_ln_gamma", [D], F32)
    b3 = dr("L0_ffn_ln_beta", [D], F32)
    q1 = dr("L0_ffn_fc1_q", [D, FF], I32)
    s1 = dr("L0_ffn_fc1_scales", [KT, FF], F32)
    fb1 = dr("L0_ffn_fc1_bias", [FF], F32)
    q2 = dr("L0_ffn_fc2_q", [FF, D], I32)
    s2 = dr("L0_ffn_fc2_scales", [FT, D], F32)
    fb2 = dr("L0_ffn_fc2_bias", [D], F32)
    out_h = nc.dram_tensor("out", [NH * HALF, D], F32, kind="ExternalOutput").ap()

    sc = lambda n, s: nc.dram_tensor(n, s, BF16, kind="Internal").ap()
    s_wk = sc("s_wk", [2, 128, 4096])
    s_wv = sc("s_wv", [2, 128, 4096])
    s_wr = sc("s_wr", [2, 128, 4096])
    s_wo = sc("s_wo", [2, 128, 4096])
    s_wfr = sc("s_wfr", [2, 128, 4096])
    s_fk = sc("s_fk", [16, 128, 2048])
    s_w1 = sc("s_w1", [16, 128, 2048])
    s_fv = sc("s_fv", [2, 16, 128, 1024])
    s_w2 = sc("s_w2", [2, 16, 128, 1024])

    TINY = {"vg1", "vb1", "vg2", "vb2", "vg3", "vb3", "vmk", "vmv", "vmr",
            "vtf", "vtd", "vfb1", "ident_sb"}

    def dma_group(buf):
        if buf in TINY:
            return "tv"
        if buf.endswith("@0"):
            return "swo0"
        if buf.endswith("@1"):
            return "swo1"
        return buf

    S = Sched(nc, dma_group=dma_group, total_groups=TINY | {"tv"})
    with contextlib.ExitStack() as st:
        T = lambda n, shp, d: st.enter_context(nc.sbuf_tensor(n, shp, d))
        x0 = T("x0", [128, 4096], F32)
        Q = T("Q", [128, 4096], F32)
        T1 = T("T1", [128, 4096], F32)
        T2 = T("T2", [128, 4096], F32)
        ztm = T("ztm", [128, 4096], BF16)
        zt2 = T("zt2", [128, 4096], BF16)
        TA = T("TA", [128, 4096], BF16)
        TB = T("TB", [128, 4096], BF16)
        bigbf = T("bigbf", [128, 16384], BF16)
        sc_ekv = T("sc_ekv", [128, 512], F32)
        sc_aa = T("sc_aa", [128, 512], F32)
        sc_bb = T("sc_bb", [128, 1024], F32)
        wsm = [T(f"wsm{i}", [128, 4096], BF16) for i in range(3)]
        fkb = T("fkb", [128, 2048], BF16)
        fvb = T("fvb", [128, 2048], BF16)
        ew_t = T("ew_t", [128, 1024], F32)
        stf = T("stf", [128, 3072], F32)
        stb = T("stb", [128, 2048], BF16)
        stq = T("stq", [128, 2048], I32)
        scbc = T("scbc", [128, 1024], F32)
        onesf = T("onesf", [1, 128], F32)
        ident_sb = T("ident_sb", [128, 128], BF16)
        vg1 = T("vg1", [128, 8], F32)
        vb1 = T("vb1", [128, 8], F32)
        vg2 = T("vg2", [128, 8], F32)
        vb2 = T("vb2", [128, 8], F32)
        vg3 = T("vg3", [128, 8], F32)
        vb3 = T("vb3", [128, 8], F32)
        vmk = T("vmk", [128, 8], F32)
        vmv = T("vmv", [128, 8], F32)
        vmr = T("vmr", [128, 8], F32)
        vtf = T("vtf", [128, 8], F32)
        vtd = T("vtd", [128, 8], F32)
        vcx = T("vcx", [128, 8], F32)
        vcy = T("vcy", [128, 8], F32)
        vca = T("vca", [128, 8], F32)
        vcb = T("vcb", [128, 8], F32)
        vfb1 = T("vfb1", [128, 32], F32)
        stats = {}
        for g in ("A", "B"):
            for nmm in ("sum", "m", "nm", "vs", "rs"):
                stats[g + nmm] = T(f"st{g}_{nmm}", [128, 4], F32)
        ce5 = T("ce5", [128, 1], F32)
        ce8 = T("ce8", [128, 1], F32)
        ce52 = T("ce52", [128, 1], F32)
        ones1 = T("ones1", [1, 128], BF16)
        b2row_b = T("b2row_b", [1, 1024], BF16)
        psA = st.enter_context(nc.psum_tensor("psA", [128, 512], F32))
        psB = st.enter_context(nc.psum_tensor("psB", [128, 512], F32))
        psT0 = st.enter_context(nc.psum_tensor("psT0", [128, 512], BF16))
        psT1 = st.enter_context(nc.psum_tensor("psT1", [128, 512], BF16))
        psO = [st.enter_context(nc.psum_tensor(f"psO{i}", [128, 512], F32))
               for i in range(4)]
        E = S.emit

        for name, dst, src_ in [("vg1", vg1, g1), ("vb1", vb1, b1),
                                ("vg2", vg2, g2), ("vb2", vb2, b2),
                                ("vg3", vg3, g3), ("vb3", vb3, b3),
                                ("vmk", vmk, mk), ("vmv", vmv, mvv),
                                ("vmr", vmr, mr), ("vtf", vtf, tf),
                                ("vtd", vtd, td)]:
            E("q_gpsimd", lambda e, dst=dst, src_=src_: e.dma_start(
                out=dst[:], in_=src_.rearrange("(k p) -> p k", p=128),
                allow_slow_non_contiguous=True),
              writes=[name], tag="ld_" + name)
        E("q_gpsimd", lambda e: e.dma_start(
            out=vfb1[:], in_=fb1.rearrange("(m p) -> p m", p=128),
            allow_slow_non_contiguous=True), writes=["vfb1"])
        E("q_gpsimd", lambda e: e.dma_start(out=ident_sb[:], in_=ident),
          writes=["ident_sb"])
        E("q_scalar", lambda e: e.dma_start(
            out=scbc[0:1, 0:1024], in_=fb2.rearrange("(a n) -> a n", a=1)),
          writes=["scbc0", "scbc1"], tag="ld_b2")
        E("vector", lambda e: e.memset(ce5[:], EPS5), writes=["ce5"])
        E("vector", lambda e: e.memset(ce8[:], EPS8), writes=["ce8"])
        E("vector", lambda e: e.memset(ce52[:], EPS5 * EPS5), writes=["ce52"])
        E("vector", lambda e: e.memset(vcx[:], 0.0), writes=["vcx"])
        E("vector", lambda e: e.memset(vcy[:], 0.0), writes=["vcy"])
        E("vector", lambda e: e.memset(vca[:], 0.0), writes=["vca"])
        E("vector", lambda e: e.memset(vcb[:], 0.0), writes=["vcb"])
        E("vector", lambda e: e.memset(ones1[:], 1.0), writes=["ones1"])
        E("vector", lambda e: e.tensor_scalar_mul(out=vg2[:], in0=vg2[:], scalar1=0.5),
          sreads=["vg2"], writes=["vg2"])
        E("vector", lambda e: e.tensor_scalar_mul(out=vb2[:], in0=vb2[:], scalar1=0.5),
          sreads=["vb2"], writes=["vb2"])
        E("vector", lambda e: e.tensor_scalar_mul(
            out=scbc[0:1, 0:1024], in0=scbc[0:1, 0:1024], scalar1=0.5),
          sreads=["scbc0", "scbc1"], writes=["scbc0", "scbc1"], tag="b2_half")
        E("vector", lambda e: e.tensor_copy(out=b2row_b[:], in_=scbc[0:1, 0:1024]),
          sreads=["scbc0", "scbc1"], writes=["b2row_b"], tag="b2_cast")
        E("scalar", lambda e: e.activation(out=vtd[:], in_=vtd[:], func=AF.Exp),
          sreads=["vtd"], writes=["vtd"])
        E("vector", lambda e: e.tensor_scalar_mul(out=vtd[:], in0=vtd[:], scalar1=-1.0),
          sreads=["vtd"], writes=["vtd"])
        E("vector", lambda e: e.memset(ew_t[:], 0.0),
          writes=["ewt0", "ewt1"], tag="ewt_init")
        E("vector", lambda e: e.memset(onesf[:], 1.0), writes=["onesf"],
          tag="onesf_init")

        su_i = [0]
        cast_jobs = []

        def cast_job(tag, in_fn, outs, wname):
            i = su_i[0]
            su_i[0] += 1
            par = i % 2
            fsl = i % 3
            fo = 1024 * fsl
            o = 1024 * par

            def emit_in():
                E(("q_gpsimd", "q_sync", "q_scalar")[fsl],
                  lambda e: e.dma_start(
                      out=stf[:, fo:fo + 1024], in_=in_fn()),
                  writes=[f"stf{fsl}"], tag=f"{tag}i")

            def emit_rest(ceng):
                if ceng == "scalar":
                    E("scalar", lambda e: e.activation(
                        out=stb[:, o:o + 1024], in_=stf[:, fo:fo + 1024],
                        func=AF.Copy),
                      reads=[f"stf{fsl}"], writes=[f"stb{par}"], tag=f"{tag}c")
                else:
                    E("vector", lambda e: e.tensor_copy(
                        out=stb[:, o:o + 1024], in_=stf[:, fo:fo + 1024]),
                      reads=[f"stf{fsl}"], writes=[f"stb{par}"], tag=f"{tag}c")
                for oi, (out_fn, so, w) in enumerate(outs):
                    E("q_gpsimd",
                      lambda e, out_fn=out_fn, so=so, w=w: e.dma_start(
                          out=out_fn(), in_=stb[:, o + so:o + so + w]),
                      reads=[f"stb{par}"], writes=[f"{wname}@{par}"],
                      tag=f"{tag}o{oi}")

            cast_jobs.append((emit_in, emit_rest))

        for wname, whbm, scr in [("s_wk", Wk, s_wk), ("s_wr", Wr, s_wr),
                                 ("s_wv", Wv, s_wv), ("s_wo", Wo, s_wo),
                                 ("s_wfr", Wfr, s_wfr)]:
            for k in range(KT):
                cast_job(f"su_{wname}{k}",
                         lambda k=k, whbm=whbm: whbm[128 * k:128 * (k + 1), :],
                         [(lambda k=k, scr=scr: scr[
                             k // 4, :, 1024 * (k % 4):1024 * (k % 4 + 1)],
                           0, 1024)],
                         wname)
        for m in range(FT):
            cast_job(f"su_fk{m}",
                     lambda m=m: Wfk[:, 128 * m:128 * (m + 1)].rearrange(
                         "(k p) m -> p k m", p=128),
                     [(lambda m=m: s_fk[m // 2, :,
                                        1024 * (m % 2):1024 * (m % 2 + 1)],
                       0, 1024)],
                     "s_fk")
        for m in range(FT):
            cast_job(f"su_fv{m}",
                     lambda m=m: Wfv[128 * m:128 * (m + 1), :],
                     [(lambda m=m, c=c: s_fv[c, m // 2, :,
                                             512 * (m % 2):512 * (m % 2) + 512],
                       512 * c, 512) for c in range(2)],
                     "s_fv")

        def pump_pair(ea="scalar", eb="vector"):
            if not cast_jobs:
                return
            a = cast_jobs.pop(0)
            b = cast_jobs.pop(0) if cast_jobs else None
            a[0]()
            if b is not None:
                b[0]()
            a[1](ea)
            if b is not None:
                b[1](eb)

        quant_fillers = []

        def make_quant(wname, qh, scr, is_w1, m):
            qi_par = m % 2
            o2 = 1024 * qi_par
            srb = f"srw{qi_par}"

            def emit_loads():
                E("q_scalar", lambda e: e.dma_start(
                    out=stf[0:1, o2:o2 + 1024].rearrange(
                        "a (k m) -> a k m", k=8) if is_w1 else
                    stf[0:1, o2:o2 + 1024],
                    in_=s1[0:8, 128 * m:128 * (m + 1)].partition_broadcast(1)
                    if is_w1 else qh_s[m:m + 1, :]),
                  writes=[srb, f"stf{qi_par}"], tag=f"su_{wname}{m}sr")
                for c in range(2):
                    sl = (2 * m + c) % 4
                    ss = slice(512 * sl, 512 * (sl + 1))
                    qb = f"stq{sl}"
                    if is_w1:
                        E("q_scalar", lambda e, c=c, ss=ss: e.dma_start(
                            out=stq[:, ss].rearrange("p (k m) -> p k m", k=4),
                            in_=qh[512 * c:512 * (c + 1),
                                   128 * m:128 * (m + 1)].rearrange(
                                "(k p) m -> p k m", p=128)),
                          writes=[qb], tag=f"su_{wname}{m}q{c}")
                    else:
                        E("q_scalar", lambda e, c=c, ss=ss: e.dma_start(
                            out=stq[:, ss],
                            in_=qh[128 * m:128 * (m + 1), 512 * c:512 * (c + 1)]),
                          writes=[qb], tag=f"su_{wname}{m}q{c}")

            def emit_rest():
                par = m % 2
                o = 1024 * par
                for c, (psx, psxn) in enumerate([(psA, "psA"), (psB, "psB")]):
                    E("tensor", lambda e, c=c, psx=psx: e.matmul(
                        psx[:], onesf[:],
                        stf[0:1, o2 + 512 * c:o2 + 512 * (c + 1)],
                        start=True, stop=True),
                      reads=[srb, "onesf"], writes=[psxn],
                      tag=f"su_{wname}{m}bc{c}")
                    E("scalar", lambda e, c=c, psx=psx: e.activation(
                        out=scbc[:, 512 * c:512 * (c + 1)], in_=psx[:],
                        func=AF.Copy),
                      reads=[psxn], writes=[f"scbc{c}"],
                      tag=f"su_{wname}{m}se{c}")
                for c in range(2):
                    sl = (2 * m + c) % 4
                    ss = slice(512 * sl, 512 * (sl + 1))
                    cs = slice(512 * c, 512 * (c + 1))
                    qb, sb = f"stq{sl}", f"scbc{c}"
                    if is_w1:
                        E("vector", lambda e, o=o, c=c, ss=ss, cs=cs: e.scalar_tensor_tensor(
                            out=stb[:, o + 512 * c:o + 512 * (c + 1)],
                            in0=stq[:, ss], scalar=-128.0,
                            in1=scbc[:, cs], op0=OP.add, op1=OP.mult),
                          reads=[qb, sb], writes=[f"stb{par}"],
                          tag=f"su_{wname}{m}d{c}")
                    else:
                        E("vector", lambda e, ss=ss: e.tensor_scalar(
                            out=zt2[:, ss], in0=stq[:, ss], scalar1=-128.0,
                            scalar2=0.5, op0=OP.add, op1=OP.mult),
                          reads=[qb], writes=["zt2"], tag=f"su_{wname}{m}dh{c}")
                        E("vector", lambda e, o=o, c=c, ss=ss, cs=cs: e.tensor_tensor(
                            out=stb[:, o + 512 * c:o + 512 * (c + 1)],
                            in0=zt2[:, ss], in1=scbc[:, cs], op=OP.mult),
                          reads=["zt2", sb], writes=[f"stb{par}"],
                          tag=f"su_{wname}{m}d{c}")
                if is_w1:
                    E("q_gpsimd", lambda e, o=o: e.dma_start(
                        out=scr[m // 2, :, 1024 * (m % 2):1024 * (m % 2 + 1)],
                        in_=stb[:, o:o + 1024]),
                      reads=[f"stb{par}"], writes=[f"{wname}@{par}"],
                      tag=f"su_{wname}{m}o")
                else:
                    for c in range(2):
                        E("q_gpsimd", lambda e, o=o, c=c: e.dma_start(
                            out=scr[c, m // 2, :, 512 * (m % 2):512 * (m % 2) + 512],
                            in_=stb[:, o + 512 * c:o + 512 * (c + 1)]),
                          reads=[f"stb{par}"], writes=[f"{wname}@{par}"],
                          tag=f"su_{wname}{m}o{c}")
            qh_s = s2 if not is_w1 else None
            return emit_loads, emit_rest

        for m in range(FT):
            quant_fillers.append(make_quant("s_w1", q1, s_w1, True, m))
        for m in range(FT):
            quant_fillers.append(make_quant("s_w2", q2, s_w2, False, m))


        def bbuf(slot):
            return f"bg{slot}"

        def ln_tile(tag, j, srcten, srcbuf, sg, dumpten, dumpbuf):
            ssum, sm, snm, svs = (stats[sg + "sum"], stats[sg + "m"],
                                  stats[sg + "nm"], stats[sg + "vs"])
            bsum, bm, bnm, bvs = (f"st{sg}sum", f"st{sg}m", f"st{sg}nm",
                                  f"st{sg}vs")
            E("vector", lambda e, j=j: e.reduce_sum(
                out=ssum[:, j:j + 1], in_=srcten[:, 1024 * j:1024 * (j + 1)],
                axis=AX.X), reads=[srcbuf], writes=[bsum], tag=f"{tag}_s{j}")
            E("vector", lambda e, j=j: e.tensor_scalar_mul(
                out=snm[:, j:j + 1], in0=ssum[:, j:j + 1], scalar1=-1.0 / D),
              sreads=[bsum], writes=[bnm], tag=f"{tag}_nm{j}")
            E("vector", lambda e, j=j: e.tensor_scalar_mul(
                out=sm[:, j:j + 1], in0=ssum[:, j:j + 1], scalar1=1.0 / D),
              sreads=[bsum], writes=[bm], tag=f"{tag}_m{j}")
            E("scalar", lambda e, j=j: e.activation(
                out=dumpten[:, 1024 * j:1024 * (j + 1)],
                in_=srcten[:, 1024 * j:1024 * (j + 1)], func=AF.Square,
                bias=snm[:, j:j + 1], accum_out=svs[:, j:j + 1]),
              reads=[srcbuf], sreads=[bnm], writes=[dumpbuf, bvs],
              tag=f"{tag}_q{j}")

        def ln_fin(tag, srcten, srcbuf, dstten, dstbuf, sg, post_cb=None):
            sm, svs, srs = stats[sg + "m"], stats[sg + "vs"], stats[sg + "rs"]
            bm, bvs, brs = f"st{sg}m", f"st{sg}vs", f"st{sg}rs"
            E("scalar", lambda e: e.activation(
                out=srs[:], in_=svs[:], func=AF.Ln, scale=1.0 / D, bias=ce5[:]),
              sreads=[bvs, "ce5"], writes=[brs], tag=f"{tag}_l")
            E("scalar", lambda e: e.activation(
                out=srs[:], in_=srs[:], func=AF.Exp, scale=-0.5),
              sreads=[brs], writes=[brs], tag=f"{tag}_e")
            for j in range(TT):
                E("vector", lambda e, j=j: e.tensor_scalar(
                    out=dstten[:, 1024 * j:1024 * (j + 1)],
                    in0=srcten[:, 1024 * j:1024 * (j + 1)],
                    scalar1=sm[:, j:j + 1], scalar2=srs[:, j:j + 1],
                    op0=OP.subtract, op1=OP.mult),
                  reads=[srcbuf], sreads=[bm, brs], writes=[dstbuf],
                  tag=f"{tag}_z{j}")
                if post_cb is not None:
                    post_cb(j)

        def emit_ln(tag, srcten, srcbuf, dstten, dstbuf, sg, dumpten, dumpbuf,
                    post_cb=None):
            for j in range(TT):
                ln_tile(tag, j, srcten, srcbuf, sg, dumpten, dumpbuf)
            ln_fin(tag, srcten, srcbuf, dstten, dstbuf, sg, post_cb)

        def emit_transpose(tag, srcten, srcbuf, dstten, dstbuf, gv, bv, gname, bname):
            for ci in range(KT):
                pst = psT0 if ci % 2 == 0 else psT1
                pbn = f"psT{ci % 2}"
                for j in range(TT):
                    E("tensor", lambda e, ci=ci, j=j, pst=pst: e.transpose(
                        pst[:, 128 * j:128 * (j + 1)],
                        srcten[:, 1024 * j + 128 * ci:1024 * j + 128 * (ci + 1)],
                        ident_sb[:]),
                      reads=[srcbuf, "ident_sb"], writes=[pbn], tag=f"{tag}_t{ci}_{j}")
                E("scalar", lambda e, ci=ci, pst=pst: e.activation(
                    out=dstten[:, 512 * ci:512 * (ci + 1)], in_=pst[:],
                    func=AF.Identity, scale=gv[:, ci:ci + 1], bias=bv[:, ci:ci + 1]),
                  reads=[pbn], sreads=[gname, bname], writes=[f"{dstbuf}{ci}"],
                  tag=f"{tag}_e{ci}")

        def stage_pre(h):
            hh = f"h{h}"
            E("q_sync", lambda e, h=h: e.dma_start(
                out=x0[:].rearrange("p (j n) -> p j n", j=4),
                in_=xcore[HALF * h:HALF * (h + 1)].rearrange(
                    "(j p) n -> p j n", p=128)),
              writes=["x0"], tag=f"{hh}_x0")
            emit_ln(f"{hh}_ln1", x0, "x0", ztm, "ztm", "A", ztm, "ztm")

        def stage_tr1(h):
            emit_transpose(f"h{h}_tr1", ztm, "ztm", TA, "TA", vg1, vb1,
                           "vg1", "vb1")

        def mix_one(h, vm, vmn, slot, upd_carry):
            hh = f"h{h}"
            for ci in range(KT):
                E("vector", lambda e, ci=ci: e.tensor_tensor(
                    out=sc_bb[:, 1:512], in0=TA[:, 512 * ci + 1:512 * (ci + 1)],
                    in1=TA[:, 512 * ci:512 * (ci + 1) - 1], op=OP.subtract),
                  reads=[f"TA{ci}"], writes=["bb0"], tag=f"{hh}_d{slot}_{ci}")
                E("vector", lambda e, ci=ci: e.tensor_tensor(
                    out=sc_bb[:, 0:1], in0=TA[:, 512 * ci:512 * ci + 1],
                    in1=vcx[:, ci:ci + 1], op=OP.subtract),
                  reads=[f"TA{ci}"], sreads=["vcx"], writes=["bb0"],
                  tag=f"{hh}_d0{slot}_{ci}")
                E("vector", lambda e, ci=ci, vm=vm: e.scalar_tensor_tensor(
                    out=bigbf[:, 512 * (slot + ci) + 1:512 * (slot + ci + 1)],
                    in0=sc_bb[:, 1:512], scalar=vm[:, ci:ci + 1],
                    in1=TA[:, 512 * ci:512 * (ci + 1) - 1],
                    op0=OP.mult, op1=OP.add),
                  reads=["bb0", f"TA{ci}"], sreads=[vmn],
                  writes=[bbuf(slot + ci)], tag=f"{hh}_mx{slot}_{ci}")
                E("vector", lambda e, ci=ci, vm=vm: e.scalar_tensor_tensor(
                    out=bigbf[:, 512 * (slot + ci):512 * (slot + ci) + 1],
                    in0=sc_bb[:, 0:1], scalar=vm[:, ci:ci + 1],
                    in1=vcx[:, ci:ci + 1], op0=OP.mult, op1=OP.add),
                  reads=["bb0"], sreads=[vmn, "vcx"], writes=[bbuf(slot + ci)],
                  tag=f"{hh}_mx0{slot}_{ci}")
                if upd_carry:
                    E("vector", lambda e, ci=ci: e.tensor_copy(
                        out=vcx[:, ci:ci + 1],
                        in_=TA[:, 512 * (ci + 1) - 1:512 * (ci + 1)]),
                      sreads=[f"TA{ci}"], writes=["vcx"], tag=f"{hh}_cx{ci}")

        def stage_mix_k(h):
            mix_one(h, vmk, "vmk", 0, False)

        def stage_mix_vr(h):
            mix_one(h, vmv, "vmv", 8, False)
            mix_one(h, vmr, "vmr", 16, True)

        KVPS = [(psA, "psA"), (psB, "psB"), (psO[0], "psO0"), (psO[1], "psO1")]
        RPS = [(psO[2], "psO2"), (psO[3], "psO3")]

        def kvr_mm(hh, mo, wlo, wlo_n, whi, whi_n, slot, tag, pool=None):
            ps, psn = (pool or KVPS)[mo % len(pool or KVPS)]
            for k in range(KT):
                wt, wn = (wlo, wlo_n) if k < 4 else (whi, whi_n)
                E("tensor", lambda e, mo=mo, k=k, ps=ps, wt=wt: e.matmul(
                    ps[:],
                    wt[:, 1024 * (k % 4) + 128 * mo:1024 * (k % 4) + 128 * (mo + 1)],
                    bigbf[:, 512 * (slot + k):512 * (slot + k + 1)],
                    start=(k == 0), stop=(k == KT - 1)),
                  reads=[wn, bbuf(slot + k)], writes=[psn], tag=f"{tag}{mo}_{k}")
            return ps, psn

        def stage_attn(h):
            hh = f"h{h}"
            E("q_sync", lambda e: e.dma_start(out=wsm[0][:], in_=s_wk[0]),
              reads=["s_wk@0", "s_wk@1"], writes=["wsm0"], tag=f"{hh}_ldk0")
            E("q_sync", lambda e: e.dma_start(out=wsm[1][:], in_=s_wk[1]),
              reads=["s_wk@0", "s_wk@1"], writes=["wsm1"], tag=f"{hh}_ldk1")
            E("q_sync", lambda e: e.dma_start(out=wsm[2][:], in_=s_wr[0]),
              reads=["s_wr@0", "s_wr@1"], writes=["wsm2"], tag=f"{hh}_ldr0")
            for mo in range(KT):
                ps, psn = kvr_mm(hh, mo, wsm[0], "wsm0", wsm[1], "wsm1", 0,
                                 f"{hh}_kmm")
                E("scalar", lambda e, mo=mo, ps=ps: e.activation(
                    out=T1[:, 512 * mo:512 * (mo + 1)], in_=ps[:], func=AF.Exp),
                  reads=[psn], writes=["T1"], tag=f"{hh}_ek{mo}")
                if h == 0:
                    E("scalar", lambda e, mo=mo, ps=ps: e.activation(
                        out=T1[:, 512 * mo:512 * mo + 1], in_=ps[:, 0:1],
                        func=AF.Exp, bias=vtf[:, mo:mo + 1]),
                      reads=[psn], sreads=["vtf"], writes=["T1"],
                      tag=f"{hh}_ektf{mo}")
                    pump_pair()
                    pump_pair()
            E("q_sync", lambda e: e.dma_start(out=wsm[0][:], in_=s_wr[1]),
              reads=["s_wr@0", "s_wr@1"], writes=["wsm0"], tag=f"{hh}_ldr1")
            E("q_sync", lambda e: e.dma_start(out=wsm[1][:], in_=s_wv[0]),
              reads=["s_wv@0", "s_wv@1"], writes=["wsm1"], tag=f"{hh}_ldv0")
            for mo in range(KT):
                ps, psn = kvr_mm(hh, mo, wsm[2], "wsm2", wsm[0], "wsm0", 16,
                                 f"{hh}_rmm", pool=RPS)
                E("scalar", lambda e, mo=mo, ps=ps: e.activation(
                    out=bigbf[:, 512 * (24 + mo):512 * (25 + mo)], in_=ps[:],
                    func=AF.Sigmoid),
                  reads=[psn], writes=[bbuf(24 + mo)], tag=f"{hh}_sg{mo}")
                sl_e = mo % 2
                ese = slice(512 * sl_e, 512 * (sl_e + 1))
                E("scalar", lambda e, mo=mo, ese=ese: e.activation(
                    out=ew_t[:, ese], in_=ew_t[:, ese], func=AF.Exp, scale=0.0,
                    bias=vtd[:, mo:mo + 1]),
                  reads=[f"ewt{sl_e}"], sreads=["vtd"], writes=[f"ewt{sl_e}"],
                  tag=f"{hh}_ew{mo}")
                bo = 512 * (mo % 2)
                bn = f"bb{mo % 2}"
                E("vector", lambda e, mo=mo, ese=ese, bo=bo: e.tensor_tensor_scan(
                    out=sc_bb[:, bo:bo + 512], data0=ew_t[:, ese],
                    data1=T1[:, 512 * mo:512 * (mo + 1)],
                    initial=vcb[:, mo:mo + 1], op0=OP.mult, op1=OP.add),
                  reads=[f"ewt{sl_e}", "T1"], sreads=["vcb"], writes=[bn],
                  tag=f"{hh}_sb{mo}")
                E("vector", lambda e, mo=mo, bo=bo: e.tensor_copy(
                    out=vcb[:, mo:mo + 1], in_=sc_bb[:, bo + 511:bo + 512]),
                  sreads=[bn], writes=["vcb"], tag=f"{hh}_cb{mo}")
                E("scalar", lambda e, bo=bo: e.activation(
                    out=sc_bb[:, bo:bo + 512], in_=sc_bb[:, bo:bo + 512],
                    func=AF.Ln, bias=ce8[:]),
                  sreads=[bn, "ce8"], writes=[bn], tag=f"{hh}_lb{mo}")
                E("scalar", lambda e, mo=mo, bo=bo: e.activation(
                    out=T2[:, 512 * mo:512 * (mo + 1)],
                    in_=sc_bb[:, bo:bo + 512], func=AF.Exp, scale=-1.0),
                  sreads=[bn], writes=["T2"], tag=f"{hh}_eb{mo}")
                if h == 0:
                    pump_pair()
                    pump_pair()
            E("q_sync", lambda e: e.dma_start(out=wsm[2][:], in_=s_wv[1]),
              reads=["s_wv@0", "s_wv@1"], writes=["wsm2"], tag=f"{hh}_ldv1")
            E("q_sync", lambda e: e.dma_start(out=wsm[0][:], in_=s_wo[0]),
              reads=["s_wo@0", "s_wo@1"], writes=["wsm0"], tag=f"{hh}_ldo0")
            for mo in range(KT):
                ps, psn = kvr_mm(hh, mo, wsm[1], "wsm1", wsm[2], "wsm2", 8,
                                 f"{hh}_vmm")
                E("vector", lambda e, mo=mo, ps=ps: e.tensor_tensor(
                    out=sc_ekv[:], in0=T1[:, 512 * mo:512 * (mo + 1)], in1=ps[:],
                    op=OP.mult),
                  reads=["T1", psn], writes=["ekv"], tag=f"{hh}_ekv{mo}")
                sl_e = mo % 2
                ese = slice(512 * sl_e, 512 * (sl_e + 1))
                E("scalar", lambda e, mo=mo, ese=ese: e.activation(
                    out=ew_t[:, ese], in_=ew_t[:, ese], func=AF.Exp, scale=0.0,
                    bias=vtd[:, mo:mo + 1]),
                  reads=[f"ewt{sl_e}"], sreads=["vtd"], writes=[f"ewt{sl_e}"],
                  tag=f"{hh}_ew2{mo}")
                E("vector", lambda e, mo=mo, ese=ese: e.tensor_tensor_scan(
                    out=sc_aa[:], data0=ew_t[:, ese],
                    data1=sc_ekv[:], initial=vca[:, mo:mo + 1],
                    op0=OP.mult, op1=OP.add),
                  reads=[f"ewt{sl_e}", "ekv"], sreads=["vca"], writes=["aa"],
                  tag=f"{hh}_sa{mo}")
                E("vector", lambda e, mo=mo: e.tensor_copy(
                    out=vca[:, mo:mo + 1], in_=sc_aa[:, 511:512]),
                  sreads=["aa"], writes=["vca"], tag=f"{hh}_ca{mo}")
                E("vector", lambda e, mo=mo: e.tensor_tensor(
                    out=T2[:, 512 * mo:512 * (mo + 1)], in0=sc_aa[:],
                    in1=T2[:, 512 * mo:512 * (mo + 1)], op=OP.mult),
                  reads=["aa", "T2"], writes=["T2"], tag=f"{hh}_wk{mo}")
                E("vector", lambda e, mo=mo: e.tensor_tensor(
                    out=bigbf[:, 512 * (24 + mo):512 * (25 + mo)],
                    in0=bigbf[:, 512 * (24 + mo):512 * (25 + mo)],
                    in1=T2[:, 512 * mo:512 * (mo + 1)], op=OP.mult),
                  reads=[bbuf(24 + mo), "T2"], writes=[bbuf(24 + mo)],
                  tag=f"{hh}_gt{mo}")
                if h == 0:
                    pump_pair("scalar", "scalar")
            E("q_sync", lambda e: e.dma_start(out=wsm[1][:], in_=s_wo[1]),
              reads=["s_wo@0", "s_wo@1"], writes=["wsm1"], tag=f"{hh}_ldo1")
            E("q_sync", lambda e: e.dma_start(out=wsm[2][:], in_=s_wfr[0]),
              reads=["s_wfr@0", "s_wfr@1"], writes=["wsm2"], tag=f"{hh}_ldf0")
            for j in range(TT):
                p0, p1 = psO[2 * (j % 2)], psO[2 * (j % 2) + 1]
                p0n, p1n = f"psO{2 * (j % 2)}", f"psO{2 * (j % 2) + 1}"
                for k in range(KT):
                    wt, wn = (wsm[0], "wsm0") if k < 4 else (wsm[1], "wsm1")
                    gsl = slice(512 * (24 + k) + 128 * j,
                                512 * (24 + k) + 128 * (j + 1))
                    E("tensor", lambda e, p0=p0, wt=wt, k=k, gsl=gsl: e.matmul(
                        p0[:], bigbf[:, gsl],
                        wt[:, 1024 * (k % 4):1024 * (k % 4) + 512],
                        start=(k == 0), stop=(k == KT - 1)),
                      reads=[bbuf(24 + k), wn], writes=[p0n],
                      tag=f"{hh}_wo{j}_{k}a")
                    E("tensor", lambda e, p1=p1, wt=wt, k=k, gsl=gsl: e.matmul(
                        p1[:], bigbf[:, gsl],
                        wt[:, 1024 * (k % 4) + 512:1024 * (k % 4) + 1024],
                        start=(k == 0), stop=(k == KT - 1)),
                      reads=[bbuf(24 + k), wn], writes=[p1n],
                      tag=f"{hh}_wo{j}_{k}b")
                for hi, pp, ppn in [(0, p0, p0n), (1, p1, p1n)]:
                    off = 1024 * j + 512 * hi
                    E("vector", lambda e, pp=pp, off=off: e.tensor_tensor(
                        out=Q[:, off:off + 512], in0=x0[:, off:off + 512],
                        in1=pp[:], op=OP.add),
                      reads=["x0", ppn], writes=["Q"], tag=f"{hh}_u{j}{hi}")
                    E("vector", lambda e, pp=pp, off=off: e.scalar_tensor_tensor(
                        out=T2[:, off:off + 512], in0=x0[:, off:off + 512],
                        scalar=2.0, in1=pp[:], op0=OP.mult, op1=OP.add),
                      reads=["x0", ppn], writes=["T2"], tag=f"{hh}_s01{j}{hi}")
                ln_tile(f"{hh}_ln2", j, Q, "Q", "B", ztm, "ztm")
                if h == 0:
                    pump_pair()
            E("q_sync", lambda e: e.dma_start(out=wsm[0][:], in_=s_wfr[1]),
              reads=["s_wfr@0", "s_wfr@1"], writes=["wsm0"], tag=f"{hh}_ldf1")
            ln_fin(f"{hh}_ln2", Q, "Q", ztm, "ztm", "B")
            emit_transpose(f"{hh}_tr2", ztm, "ztm", TA, "TA", vg2, vb2,
                           "vg2", "vb2")
            for ci in range(KT):
                E("vector", lambda e, ci=ci: e.tensor_tensor(
                    out=TB[:, 512 * ci + 1:512 * (ci + 1)],
                    in0=TA[:, 512 * ci + 1:512 * (ci + 1)],
                    in1=TA[:, 512 * ci:512 * (ci + 1) - 1], op=OP.add),
                  reads=[f"TA{ci}"], writes=[f"TB{ci}"], tag=f"{hh}_ym{ci}")
                E("vector", lambda e, ci=ci: e.tensor_tensor(
                    out=TB[:, 512 * ci:512 * ci + 1],
                    in0=TA[:, 512 * ci:512 * ci + 1], in1=vcy[:, ci:ci + 1],
                    op=OP.add),
                  reads=[f"TA{ci}"], sreads=["vcy"], writes=[f"TB{ci}"],
                  tag=f"{hh}_ym0{ci}")
                E("vector", lambda e, ci=ci: e.tensor_copy(
                    out=vcy[:, ci:ci + 1],
                    in_=TA[:, 512 * (ci + 1) - 1:512 * (ci + 1)]),
                  sreads=[f"TA{ci}"], writes=["vcy"], tag=f"{hh}_cy{ci}")
            for j in range(TT):
                p0, p1 = psO[2 * (j % 2)], psO[2 * (j % 2) + 1]
                p0n, p1n = f"psO{2 * (j % 2)}", f"psO{2 * (j % 2) + 1}"
                for k in range(KT):
                    wt, wn = (wsm[2], "wsm2") if k < 4 else (wsm[0], "wsm0")
                    asl = slice(512 * k + 128 * j, 512 * k + 128 * (j + 1))
                    E("tensor", lambda e, p0=p0, wt=wt, k=k, asl=asl: e.matmul(
                        p0[:], TA[:, asl],
                        wt[:, 1024 * (k % 4):1024 * (k % 4) + 512],
                        start=(k == 0), stop=(k == KT - 1)),
                      reads=[f"TA{k}", wn], writes=[p0n],
                      tag=f"{hh}_fr{j}_{k}a")
                    E("tensor", lambda e, p1=p1, wt=wt, k=k, asl=asl: e.matmul(
                        p1[:], TA[:, asl],
                        wt[:, 1024 * (k % 4) + 512:1024 * (k % 4) + 1024],
                        start=(k == 0), stop=(k == KT - 1)),
                      reads=[f"TA{k}", wn], writes=[p1n],
                      tag=f"{hh}_fr{j}_{k}b")
                for hi, pp, ppn in [(0, p0, p0n), (1, p1, p1n)]:
                    off = 1024 * j + 512 * hi
                    E("scalar", lambda e, pp=pp, off=off: e.activation(
                        out=T1[:, off:off + 512], in_=pp[:], func=AF.Sigmoid,
                        scale=2.0),
                      reads=[ppn], writes=["T1"], tag=f"{hh}_rr{j}{hi}")

        def emit_mm2(tagp, mp, cho, bias_close):
            mm, mpar = mp
            sl = (mm // 2) % 2
            last = (mm == FT - 1)
            for j in range(TT):
                E("tensor", lambda e, mm=mm, j=j, mpar=mpar, sl=sl: e.matmul(
                    psO[j][:],
                    bigbf[:, 512 * mm + 128 * j:512 * mm + 128 * (j + 1)],
                    fvb[:, 1024 * sl + 512 * mpar:1024 * sl + 512 * (mpar + 1)],
                    start=(mm == 0), stop=(last and not bias_close)),
                  reads=[bbuf(mm), f"fvb{2 * sl + mpar}"], writes=[f"psO{j}"],
                  tag=f"{tagp}m2_{cho}_{mm}_{j}")

        def ffn_pair(tagp, blkn, rown, scr_blk, scr_row, evac_fn, comb_fn,
                     bias_close, fillers=None):
            for cho in range(2):
                pend = []
                for m in range(FT):
                    par = m % 2
                    sl = (m // 2) % 2
                    if cho == 0:
                        E("q_sync", lambda e, m=m, par=par: e.dma_start(
                            out=fkb[:, 1024 * par:1024 * (par + 1)],
                            in_=scr_blk[m // 2, :,
                                        1024 * (m % 2):1024 * (m % 2 + 1)]),
                          reads=[blkn + "@0", blkn + "@1"],
                          writes=[f"fkb{par}"], tag=f"{tagp}b{m}")
                    if par == 0:
                        E("q_sync", lambda e, m=m, cho=cho, sl=sl: e.dma_start(
                            out=fvb[:, 1024 * sl:1024 * (sl + 1)],
                            in_=scr_row[cho, m // 2]),
                          reads=[rown + "@0", rown + "@1"],
                          writes=[f"fvb{2 * sl}", f"fvb{2 * sl + 1}"],
                          tag=f"{tagp}r{m}_{cho}")
                    if cho == 0:
                        ps, psn = (psA, "psA") if par == 0 else (psB, "psB")
                        for k in range(KT):
                            E("tensor", lambda e, m=m, k=k, ps=ps, par=par: e.matmul(
                                ps[:],
                                fkb[:, 1024 * par + 128 * k:1024 * par + 128 * (k + 1)],
                                TB[:, 512 * k:512 * (k + 1)],
                                start=(k == 0), stop=(k == KT - 1)),
                              reads=[f"fkb{par}", f"TB{k}"], writes=[psn],
                              tag=f"{tagp}m1_{m}_{k}")
                        evac_fn(m, ps, psn, par)
                    if fillers:
                        for f in fillers(m, cho):
                            f()
                    pend.append((m, par))
                    if len(pend) >= 2:
                        emit_mm2(tagp, pend.pop(0), cho, bias_close)
                while pend:
                    emit_mm2(tagp, pend.pop(0), cho, bias_close)
                comb_fn(cho)

        def stage_fA(h, fillers=None):
            hh = f"h{h}"

            def kk_evac(m, ps, psn, par):
                E("scalar", lambda e, ps=ps, m=m: e.activation(
                    out=bigbf[:, 512 * m:512 * (m + 1)], in_=ps[:],
                    func=AF.Relu),
                  reads=[psn], writes=[bbuf(m)], tag=f"{hh}_re{m}")
                E("vector", lambda e, m=m: e.tensor_tensor(
                    out=bigbf[:, 512 * m:512 * (m + 1)],
                    in0=bigbf[:, 512 * m:512 * (m + 1)],
                    in1=bigbf[:, 512 * m:512 * (m + 1)], op=OP.mult),
                  reads=[bbuf(m)], writes=[bbuf(m)], tag=f"{hh}_sq{m}")

            def comb1(cho):
                for j in range(TT):
                    off = 1024 * j + 512 * cho
                    E("vector", lambda e, j=j, off=off: e.tensor_tensor(
                        out=sc_ekv[:], in0=T1[:, off:off + 512], in1=psO[j][:],
                        op=OP.mult),
                      reads=["T1", f"psO{j}"], writes=["ekv"],
                      tag=f"{hh}_g1{cho}{j}")
                    E("vector", lambda e, off=off: e.tensor_tensor(
                        out=T2[:, off:off + 512], in0=T2[:, off:off + 512],
                        in1=sc_ekv[:], op=OP.add),
                      reads=["ekv", "T2"], writes=["T2"], tag=f"{hh}_a1{cho}{j}")
                    if cho == 1:
                        ln_tile(f"{hh}_pl1", j, T2, "T2", "B", zt2, "zt2")

            ffn_pair(f"{hh}_fA", "s_fk", "s_fv", s_fk, s_fv, kk_evac, comb1,
                     False, fillers)

        def stage_mid(h):
            hh = f"h{h}"
            svs = stats["Bvs"]
            sS = stats["Ars"]
            sm = stats["Bm"]
            E("scalar", lambda e: e.activation(
                out=sS[:], in_=svs[:], func=AF.Ln, scale=(1.0 + EPS5) / D,
                bias=ce52[:]),
              sreads=["stBvs", "ce52"], writes=["stArs"], tag=f"{hh}_r5")
            E("scalar", lambda e: e.activation(
                out=sS[:], in_=sS[:], func=AF.Exp, scale=-0.5),
              sreads=["stArs"], writes=["stArs"], tag=f"{hh}_r6")
            for j in range(TT):
                E("vector", lambda e, j=j: e.tensor_scalar(
                    out=zt2[:, 1024 * j:1024 * (j + 1)],
                    in0=T2[:, 1024 * j:1024 * (j + 1)],
                    scalar1=sm[:, j:j + 1], scalar2=sS[:, j:j + 1],
                    op0=OP.subtract, op1=OP.mult),
                  reads=["T2"], sreads=["stBm", "stArs"], writes=["zt2"],
                  tag=f"{hh}_z3{j}")
            emit_transpose(f"{hh}_tr3", zt2, "zt2", TB, "TB", vg3, vb3,
                           "vg3", "vb3")
            ln_fin(f"{hh}_pl1", T2, "T2", Q, "Q", "B")

        def stage_fB(h):
            hh = f"h{h}"

            def silu_evac(m, ps, psn, par):
                E("scalar", lambda e, m=m, ps=ps: e.activation(
                    out=bigbf[:, 512 * m:512 * (m + 1)], in_=ps[:], func=AF.Silu,
                    bias=vfb1[:, m:m + 1]),
                  reads=[psn], sreads=["vfb1"], writes=[bbuf(m)],
                  tag=f"{hh}_si{m}")

            def comb2(cho):
                for j in range(TT):
                    E("tensor", lambda e, j=j, cho=cho: e.matmul(
                        psO[j][:], ones1[:],
                        b2row_b[0:1, 512 * cho:512 * (cho + 1)],
                        start=False, stop=True),
                      reads=["ones1", "b2row_b"], writes=[f"psO{j}"],
                      tag=f"{hh}_bz{cho}{j}")
                    off = 1024 * j + 512 * cho
                    E("vector", lambda e, j=j, off=off: e.scalar_tensor_tensor(
                        out=T2[:, off:off + 512], in0=Q[:, off:off + 512],
                        scalar=2.0, in1=psO[j][:], op0=OP.mult, op1=OP.add),
                      reads=["Q", f"psO{j}"], writes=["T2"],
                      tag=f"{hh}_c2{cho}{j}")

            ffn_pair(f"{hh}_fB", "s_w1", "s_w2", s_w1, s_w2, silu_evac, comb2,
                     True, fillers=mk_fB_fillers(h))

        def stage_tail(h):
            hh = f"h{h}"
            for j in range(TT):
                ln_tile(f"{hh}_pl2", j, T2, "T2", "B", zt2, "zt2")
            ln_fin(f"{hh}_pl2", T2, "T2", Q, "Q", "B")
            E("q_gpsimd", lambda e, h=h: e.dma_start(
                out=out_h[HALF * h:HALF * (h + 1)].rearrange(
                    "(j p) n -> p j n", p=128),
                in_=Q[:].rearrange("p (j n) -> p j n", j=4)),
              reads=["Q"], writes=[f"od{h}"], tag=f"{hh}_out")

        def mk_fillers(h):
            def fillers(m, cho):
                out = []
                if cho != 0:
                    return out
                if h == 0:
                    for qi in (2 * m, 2 * m + 1):
                        if qi < len(quant_fillers):
                            out.append(quant_fillers[qi][1])
                    for qi in (2 * m + 2, 2 * m + 3):
                        if qi < len(quant_fillers):
                            out.append(quant_fillers[qi][0])
                if m == 4 and h + 1 < NH:
                    out.append(lambda: stage_pre(h + 1))
                return out
            return fillers

        def mk_fB_fillers(h):
            def fillers(m, cho):
                if cho == 1 and m == 10 and h + 1 < NH:
                    return [lambda: stage_mix_k(h + 1)]
                return []
            return fillers

        pump_pair()
        pump_pair()
        stage_pre(0)
        pump_pair()
        pump_pair()
        stage_tr1(0)
        pump_pair()
        pump_pair()
        stage_mix_k(0)
        pump_pair()
        pump_pair()
        for h in range(NH):
            stage_mix_vr(h)
            stage_attn(h)
            while cast_jobs:
                pump_pair()
            if h == 0:
                quant_fillers[0][0]()
                quant_fillers[1][0]()
            stage_fA(h, fillers=mk_fillers(h))
            if h + 1 < NH:
                stage_tr1(h + 1)
            stage_mid(h)
            stage_fB(h)
            stage_tail(h)

        E("gpsimd", lambda e: None, reads=[f"od{h}" for h in range(NH)],
          tag="final_wait")
        S.build()
        sems = {}
        for key in sorted({it["skey"] for it in S.items}):
            sems[key] = st.enter_context(nc.semaphore("sem_" + key))
        S.sems = sems
        block = st.enter_context(nc.Block())
        S.materialize(block)
    return nc


_NC_CACHE = None


def kernel(**inputs):
    global _NC_CACHE
    x = np.ascontiguousarray(inputs["L0_attn_x"])
    if _NC_CACHE is None:
        _NC_CACHE = build_program()
    nc = _NC_CACHE
    ident = np.eye(128, dtype=ml_dtypes.bfloat16)
    base = {k: np.ascontiguousarray(v) for k, v in inputs.items()
            if k != "L0_attn_x"}
    in_maps = []
    for i in range(8):
        m = dict(base)
        m["xcore"] = np.ascontiguousarray(x[i])
        m["ident"] = ident
        in_maps.append(m)
    res = run_bass_kernel_spmd(nc, in_maps, core_ids=list(range(8)))
    out = np.stack([res.results[i]["out"] for i in range(8)], axis=0)
    return out.astype(np.float32)


# version 26
# speedup vs baseline: 1.0177x; 1.0177x over previous
import contextlib
import numpy as np
import ml_dtypes
import concourse.bass as bass
import concourse.mybir as mybir
from concourse.bass_utils import run_bass_kernel_spmd

F32 = mybir.dt.float32
BF16 = mybir.dt.bfloat16
I32 = mybir.dt.int32
AF = mybir.ActivationFunctionType
OP = mybir.AluOpType
AX = mybir.AxisListType

D = 1024
KT = 8
HALF = 512
NH = 8
TT = 4
FF = 4096
FT = 32
EPS5 = 1e-5
EPS8 = 1e-8

COMPUTE_ENGINES = ("tensor", "vector", "scalar", "gpsimd")
DMA_QUEUES = {"q_sync": "sync", "q_gpsimd": "gpsimd", "q_scalar": "scalar",
              "q_vector": "vector"}


class Buf:
    __slots__ = ("name", "last_write", "readers")

    def __init__(self, name):
        self.name = name
        self.last_write = None
        self.readers = []


class Sched:

    def __init__(self, nc, dma_group=None, total_groups=()):
        self.nc = nc
        self.items = []
        self._bufs = {}
        self.sems = None
        self.dma_group = dma_group or (lambda buf: buf)
        self.total_groups = set(total_groups)

    def buf(self, name):
        b = self._bufs.get(name)
        if b is None:
            b = self._bufs[name] = Buf(name)
        return b

    def emit(self, stream, fn, reads=(), writes=(), sreads=(), tag=""):
        idx = len(self.items)
        reads = [self.buf(r) if isinstance(r, str) else r for r in reads]
        sreads = [self.buf(r) if isinstance(r, str) else r for r in sreads]
        writes = [self.buf(w) if isinstance(w, str) else w for w in writes]
        rdeps = set()
        wdeps = set()
        hard = set()
        for b in reads:
            if b.last_write is not None:
                rdeps.add(b.last_write)
        for b in sreads:
            if b.last_write is not None:
                hard.add(b.last_write)
        for b in writes:
            if b.last_write is not None:
                wdeps.add(b.last_write)
            for r in b.readers:
                wdeps.add(r)
        if stream in DMA_QUEUES:
            assert writes, f"DMA item {tag} needs a write buf for its semaphore"
            skey = "d_" + self.dma_group(writes[0].name)
        else:
            skey = stream
        self.items.append(dict(idx=idx, stream=stream, fn=fn, rdeps=rdeps,
                               wdeps=wdeps, hard=hard, tag=tag, signals=False,
                               skey=skey, sem_val=None, waits=None))
        for b in list(reads) + list(sreads):
            b.readers.append(idx)
        for b in writes:
            b.last_write = idx
            b.readers = []
        return idx

    def build(self):
        items = self.items
        for it in items:
            if it["stream"] in DMA_QUEUES:
                it["signals"] = True
            for d in list(it["rdeps"]) + list(it["wdeps"]):
                dep = items[d]
                if dep["stream"] == "tensor" and it["stream"] == "tensor":
                    continue
                dep["signals"] = True
            for d in it["hard"]:
                items[d]["signals"] = True
        counters = {}
        for it in items:
            if it["signals"]:
                step = 16 if it["stream"] in DMA_QUEUES else 1
                counters[it["skey"]] = counters.get(it["skey"], 0) + step
                it["sem_val"] = counters[it["skey"]]
        self.sem_totals = dict(counters)
        for it in items:
            need = {}

            def add(dep, force_total):
                s, v = dep["skey"], dep["sem_val"]
                assert v is not None
                if force_total and s.startswith("d_") and s[2:] in self.total_groups:
                    v = self.sem_totals[s]
                if need.get(s, 0) < v:
                    need[s] = v
            for d in it["rdeps"]:
                dep = items[d]
                if dep["stream"] == "tensor" and it["stream"] == "tensor":
                    continue
                add(dep, True)
            for d in it["wdeps"]:
                dep = items[d]
                if dep["stream"] == "tensor" and it["stream"] == "tensor":
                    continue
                if (dep["skey"] == it["skey"] and it["stream"] in DMA_QUEUES
                        and it["skey"].startswith("d_")
                        and it["skey"][2:] in self.total_groups):
                    continue
                add(dep, False)
            for d in it["hard"]:
                add(items[d], True)
            it["waits"] = need
        seen = {}
        for it in items:
            ee = DMA_QUEUES.get(it["stream"], it["stream"])
            got = seen.setdefault(ee, {})
            w = {}
            for s, v in sorted(it["waits"].items()):
                if got.get(s, 0) < v:
                    w[s] = v
                    got[s] = v
            it["waits"] = w

    def materialize(self, block):
        items = self.items
        sems = self.sems

        def run_stream(eng_name, eng):
            for it in items:
                if DMA_QUEUES.get(it["stream"], it["stream"]) != eng_name:
                    continue
                for s, v in sorted(it["waits"].items()):
                    eng.wait_ge(sems[s], v)
                inst = it["fn"](eng)
                if it["signals"]:
                    step = 16 if it["stream"] in DMA_QUEUES else 1
                    assert inst is not None, f"item {it['idx']} {it['tag']}"
                    inst.then_inc(sems[it["skey"]], step)

        for name in list(COMPUTE_ENGINES) + ["sync"]:
            if any(DMA_QUEUES.get(it["stream"], it["stream"]) == name
                   for it in self.items):
                getattr(block, name)(lambda eng, n=name: run_stream(n, eng))


def build_program():
    nc = bass.Bass("TRN2", target_bir_lowering=False)
    dr = lambda n, s, d: nc.dram_tensor(n, s, d, kind="ExternalInput").ap()
    xcore = dr("xcore", [NH * HALF, D], F32)
    ident = dr("ident", [128, 128], BF16)
    Wk = dr("L0_attn_Wk", [D, D], F32)
    Wv = dr("L0_attn_Wv", [D, D], F32)
    Wr = dr("L0_attn_Wr", [D, D], F32)
    Wo = dr("L0_attn_Wo", [D, D], F32)
    Wfr = dr("L0_attn_Wfr", [D, D], F32)
    Wfk = dr("L0_attn_Wfk", [D, FF], F32)
    Wfv = dr("L0_attn_Wfv", [FF, D], F32)
    g1 = dr("L0_attn_ln1_g", [D], F32)
    b1 = dr("L0_attn_ln1_b", [D], F32)
    g2 = dr("L0_attn_ln2_g", [D], F32)
    b2 = dr("L0_attn_ln2_b", [D], F32)
    td = dr("L0_attn_time_decay", [D], F32)
    tf = dr("L0_attn_time_first", [D], F32)
    mk = dr("L0_attn_mix_k", [D], F32)
    mvv = dr("L0_attn_mix_v", [D], F32)
    mr = dr("L0_attn_mix_r", [D], F32)
    g3 = dr("L0_ffn_ln_gamma", [D], F32)
    b3 = dr("L0_ffn_ln_beta", [D], F32)
    q1 = dr("L0_ffn_fc1_q", [D, FF], I32)
    s1 = dr("L0_ffn_fc1_scales", [KT, FF], F32)
    fb1 = dr("L0_ffn_fc1_bias", [FF], F32)
    q2 = dr("L0_ffn_fc2_q", [FF, D], I32)
    s2 = dr("L0_ffn_fc2_scales", [FT, D], F32)
    fb2 = dr("L0_ffn_fc2_bias", [D], F32)
    out_h = nc.dram_tensor("out", [NH * HALF, D], F32, kind="ExternalOutput").ap()

    sc = lambda n, s: nc.dram_tensor(n, s, BF16, kind="Internal").ap()
    s_wk = sc("s_wk", [2, 128, 4096])
    s_wv = sc("s_wv", [2, 128, 4096])
    s_wr = sc("s_wr", [2, 128, 4096])
    s_wo = sc("s_wo", [2, 128, 4096])
    s_wfr = sc("s_wfr", [2, 128, 4096])
    s_fk = sc("s_fk", [16, 128, 2048])
    s_w1 = sc("s_w1", [16, 128, 2048])
    s_fv = sc("s_fv", [2, 16, 128, 1024])
    s_w2 = sc("s_w2", [2, 16, 128, 1024])

    TINY = {"vg1", "vb1", "vg2", "vb2", "vg3", "vb3", "vmk", "vmv", "vmr",
            "vtf", "vtd", "vfb1", "ident_sb"}

    def dma_group(buf):
        if buf in TINY:
            return "tv"
        if buf.endswith("@0"):
            return "swo0"
        if buf.endswith("@1"):
            return "swo1"
        return buf

    S = Sched(nc, dma_group=dma_group, total_groups=TINY | {"tv"})
    with contextlib.ExitStack() as st:
        T = lambda n, shp, d: st.enter_context(nc.sbuf_tensor(n, shp, d))
        x0 = T("x0", [128, 4096], F32)
        Q = T("Q", [128, 4096], F32)
        T1 = T("T1", [128, 4096], F32)
        T2 = T("T2", [128, 4096], F32)
        ztm = T("ztm", [128, 4096], BF16)
        zt2 = T("zt2", [128, 4096], BF16)
        TA = T("TA", [128, 4096], BF16)
        TB = T("TB", [128, 4096], BF16)
        bigbf = T("bigbf", [128, 16384], BF16)
        sc_ekv = T("sc_ekv", [128, 512], F32)
        sc_aa = T("sc_aa", [128, 512], F32)
        sc_bb = T("sc_bb", [128, 1024], F32)
        rsg = T("rsg", [128, 512], BF16)
        wsm = [T(f"wsm{i}", [128, 4096], BF16) for i in range(3)]
        fkb = T("fkb", [128, 2048], BF16)
        fvb = T("fvb", [128, 2048], BF16)
        ew_t = T("ew_t", [128, 1024], F32)
        stf = T("stf", [128, 3072], F32)
        stb = T("stb", [128, 2048], BF16)
        stq = T("stq", [128, 2048], I32)
        scbc = T("scbc", [128, 1024], F32)
        onesf = T("onesf", [1, 128], F32)
        ident_sb = T("ident_sb", [128, 128], BF16)
        vg1 = T("vg1", [128, 8], F32)
        vb1 = T("vb1", [128, 8], F32)
        vg2 = T("vg2", [128, 8], F32)
        vb2 = T("vb2", [128, 8], F32)
        vg3 = T("vg3", [128, 8], F32)
        vb3 = T("vb3", [128, 8], F32)
        vmk = T("vmk", [128, 8], F32)
        vmv = T("vmv", [128, 8], F32)
        vmr = T("vmr", [128, 8], F32)
        vtf = T("vtf", [128, 8], F32)
        vtd = T("vtd", [128, 8], F32)
        vcx = T("vcx", [128, 8], F32)
        vcy = T("vcy", [128, 8], F32)
        vca = T("vca", [128, 8], F32)
        vcb = T("vcb", [128, 8], F32)
        vfb1 = T("vfb1", [128, 32], F32)
        stats = {}
        for g in ("A", "B"):
            for nmm in ("sum", "m", "nm", "vs", "rs"):
                stats[g + nmm] = T(f"st{g}_{nmm}", [128, 4], F32)
        ce5 = T("ce5", [128, 1], F32)
        ce8 = T("ce8", [128, 1], F32)
        ce52 = T("ce52", [128, 1], F32)
        ones1 = T("ones1", [1, 128], BF16)
        b2row_b = T("b2row_b", [1, 1024], BF16)
        psA = st.enter_context(nc.psum_tensor("psA", [128, 512], F32))
        psB = st.enter_context(nc.psum_tensor("psB", [128, 512], F32))
        psT0 = st.enter_context(nc.psum_tensor("psT0", [128, 512], BF16))
        psT1 = st.enter_context(nc.psum_tensor("psT1", [128, 512], BF16))
        psO = [st.enter_context(nc.psum_tensor(f"psO{i}", [128, 512], F32))
               for i in range(4)]
        E = S.emit

        for name, dst, src_ in [("vg1", vg1, g1), ("vb1", vb1, b1),
                                ("vg2", vg2, g2), ("vb2", vb2, b2),
                                ("vg3", vg3, g3), ("vb3", vb3, b3),
                                ("vmk", vmk, mk), ("vmv", vmv, mvv),
                                ("vmr", vmr, mr), ("vtf", vtf, tf),
                                ("vtd", vtd, td)]:
            E("q_gpsimd", lambda e, dst=dst, src_=src_: e.dma_start(
                out=dst[:], in_=src_.rearrange("(k p) -> p k", p=128),
                allow_slow_non_contiguous=True),
              writes=[name], tag="ld_" + name)
        E("q_gpsimd", lambda e: e.dma_start(
            out=vfb1[:], in_=fb1.rearrange("(m p) -> p m", p=128),
            allow_slow_non_contiguous=True), writes=["vfb1"])
        E("q_gpsimd", lambda e: e.dma_start(out=ident_sb[:], in_=ident),
          writes=["ident_sb"])
        E("q_scalar", lambda e: e.dma_start(
            out=scbc[0:1, 0:1024], in_=fb2.rearrange("(a n) -> a n", a=1)),
          writes=["scbc0", "scbc1"], tag="ld_b2")
        E("vector", lambda e: e.memset(ce5[:], EPS5), writes=["ce5"])
        E("vector", lambda e: e.memset(ce8[:], EPS8), writes=["ce8"])
        E("vector", lambda e: e.memset(ce52[:], EPS5 * EPS5), writes=["ce52"])
        E("vector", lambda e: e.memset(vcx[:], 0.0), writes=["vcx"])
        E("vector", lambda e: e.memset(vcy[:], 0.0), writes=["vcy"])
        E("vector", lambda e: e.memset(vca[:], 0.0), writes=["vca"])
        E("vector", lambda e: e.memset(vcb[:], 0.0), writes=["vcb"])
        E("vector", lambda e: e.memset(ones1[:], 1.0), writes=["ones1"])
        E("vector", lambda e: e.tensor_scalar_mul(out=vg2[:], in0=vg2[:], scalar1=0.5),
          sreads=["vg2"], writes=["vg2"])
        E("vector", lambda e: e.tensor_scalar_mul(out=vb2[:], in0=vb2[:], scalar1=0.5),
          sreads=["vb2"], writes=["vb2"])
        E("vector", lambda e: e.tensor_scalar_mul(
            out=scbc[0:1, 0:1024], in0=scbc[0:1, 0:1024], scalar1=0.5),
          sreads=["scbc0", "scbc1"], writes=["scbc0", "scbc1"], tag="b2_half")
        E("vector", lambda e: e.tensor_copy(out=b2row_b[:], in_=scbc[0:1, 0:1024]),
          sreads=["scbc0", "scbc1"], writes=["b2row_b"], tag="b2_cast")
        E("scalar", lambda e: e.activation(out=vtd[:], in_=vtd[:], func=AF.Exp),
          sreads=["vtd"], writes=["vtd"])
        E("vector", lambda e: e.tensor_scalar_mul(out=vtd[:], in0=vtd[:], scalar1=-1.0),
          sreads=["vtd"], writes=["vtd"])
        E("vector", lambda e: e.memset(ew_t[:], 0.0),
          writes=["ewt0", "ewt1"], tag="ewt_init")
        E("vector", lambda e: e.memset(onesf[:], 1.0), writes=["onesf"],
          tag="onesf_init")

        su_i = [0]
        cast_jobs = []

        def cast_job(tag, in_fn, outs, wname):
            i = su_i[0]
            su_i[0] += 1
            par = i % 2
            fsl = i % 3
            fo = 1024 * fsl
            o = 1024 * par

            def emit_in():
                E(("q_gpsimd", "q_sync", "q_scalar")[fsl],
                  lambda e: e.dma_start(
                      out=stf[:, fo:fo + 1024], in_=in_fn()),
                  writes=[f"stf{fsl}"], tag=f"{tag}i")

            def emit_rest(ceng):
                if ceng == "scalar":
                    E("scalar", lambda e: e.activation(
                        out=stb[:, o:o + 1024], in_=stf[:, fo:fo + 1024],
                        func=AF.Copy),
                      reads=[f"stf{fsl}"], writes=[f"stb{par}"], tag=f"{tag}c")
                else:
                    E("vector", lambda e: e.tensor_copy(
                        out=stb[:, o:o + 1024], in_=stf[:, fo:fo + 1024]),
                      reads=[f"stf{fsl}"], writes=[f"stb{par}"], tag=f"{tag}c")
                for oi, (out_fn, so, w) in enumerate(outs):
                    E("q_gpsimd",
                      lambda e, out_fn=out_fn, so=so, w=w: e.dma_start(
                          out=out_fn(), in_=stb[:, o + so:o + so + w]),
                      reads=[f"stb{par}"], writes=[f"{wname}@{par}"],
                      tag=f"{tag}o{oi}")

            cast_jobs.append((emit_in, emit_rest))

        for wname, whbm, scr in [("s_wk", Wk, s_wk), ("s_wv", Wv, s_wv),
                                 ("s_wr", Wr, s_wr), ("s_wo", Wo, s_wo),
                                 ("s_wfr", Wfr, s_wfr)]:
            for k in range(KT):
                cast_job(f"su_{wname}{k}",
                         lambda k=k, whbm=whbm: whbm[128 * k:128 * (k + 1), :],
                         [(lambda k=k, scr=scr: scr[
                             k // 4, :, 1024 * (k % 4):1024 * (k % 4 + 1)],
                           0, 1024)],
                         wname)
        for m in range(FT):
            cast_job(f"su_fk{m}",
                     lambda m=m: Wfk[:, 128 * m:128 * (m + 1)].rearrange(
                         "(k p) m -> p k m", p=128),
                     [(lambda m=m: s_fk[m // 2, :,
                                        1024 * (m % 2):1024 * (m % 2 + 1)],
                       0, 1024)],
                     "s_fk")
        for m in range(FT):
            cast_job(f"su_fv{m}",
                     lambda m=m: Wfv[128 * m:128 * (m + 1), :],
                     [(lambda m=m, c=c: s_fv[c, m // 2, :,
                                             512 * (m % 2):512 * (m % 2) + 512],
                       512 * c, 512) for c in range(2)],
                     "s_fv")

        def pump_pair(ea="scalar", eb="vector"):
            if not cast_jobs:
                return
            a = cast_jobs.pop(0)
            b = cast_jobs.pop(0) if cast_jobs else None
            a[0]()
            if b is not None:
                b[0]()
            a[1](ea)
            if b is not None:
                b[1](eb)

        quant_fillers = []

        def make_quant(wname, qh, scr, is_w1, m):
            qi_par = m % 2
            o2 = 1024 * qi_par
            srb = f"srw{qi_par}"

            def emit_loads():
                E("q_scalar", lambda e: e.dma_start(
                    out=stf[0:1, o2:o2 + 1024].rearrange(
                        "a (k m) -> a k m", k=8) if is_w1 else
                    stf[0:1, o2:o2 + 1024],
                    in_=s1[0:8, 128 * m:128 * (m + 1)].partition_broadcast(1)
                    if is_w1 else qh_s[m:m + 1, :]),
                  writes=[srb, f"stf{qi_par}"], tag=f"su_{wname}{m}sr")
                for c in range(2):
                    sl = (2 * m + c) % 4
                    ss = slice(512 * sl, 512 * (sl + 1))
                    qb = f"stq{sl}"
                    if is_w1:
                        E("q_scalar", lambda e, c=c, ss=ss: e.dma_start(
                            out=stq[:, ss].rearrange("p (k m) -> p k m", k=4),
                            in_=qh[512 * c:512 * (c + 1),
                                   128 * m:128 * (m + 1)].rearrange(
                                "(k p) m -> p k m", p=128)),
                          writes=[qb], tag=f"su_{wname}{m}q{c}")
                    else:
                        E("q_scalar", lambda e, c=c, ss=ss: e.dma_start(
                            out=stq[:, ss],
                            in_=qh[128 * m:128 * (m + 1), 512 * c:512 * (c + 1)]),
                          writes=[qb], tag=f"su_{wname}{m}q{c}")

            def emit_rest():
                par = m % 2
                o = 1024 * par
                for c, (psx, psxn) in enumerate([(psA, "psA"), (psB, "psB")]):
                    E("tensor", lambda e, c=c, psx=psx: e.matmul(
                        psx[:], onesf[:],
                        stf[0:1, o2 + 512 * c:o2 + 512 * (c + 1)],
                        start=True, stop=True),
                      reads=[srb, "onesf"], writes=[psxn],
                      tag=f"su_{wname}{m}bc{c}")
                    E("scalar", lambda e, c=c, psx=psx: e.activation(
                        out=scbc[:, 512 * c:512 * (c + 1)], in_=psx[:],
                        func=AF.Copy),
                      reads=[psxn], writes=[f"scbc{c}"],
                      tag=f"su_{wname}{m}se{c}")
                for c in range(2):
                    sl = (2 * m + c) % 4
                    ss = slice(512 * sl, 512 * (sl + 1))
                    cs = slice(512 * c, 512 * (c + 1))
                    qb, sb = f"stq{sl}", f"scbc{c}"
                    if is_w1:
                        E("vector", lambda e, o=o, c=c, ss=ss, cs=cs: e.scalar_tensor_tensor(
                            out=stb[:, o + 512 * c:o + 512 * (c + 1)],
                            in0=stq[:, ss], scalar=-128.0,
                            in1=scbc[:, cs], op0=OP.add, op1=OP.mult),
                          reads=[qb, sb], writes=[f"stb{par}"],
                          tag=f"su_{wname}{m}d{c}")
                    else:
                        E("vector", lambda e, ss=ss: e.tensor_scalar(
                            out=zt2[:, ss], in0=stq[:, ss], scalar1=-128.0,
                            scalar2=0.5, op0=OP.add, op1=OP.mult),
                          reads=[qb], writes=["zt2"], tag=f"su_{wname}{m}dh{c}")
                        E("vector", lambda e, o=o, c=c, ss=ss, cs=cs: e.tensor_tensor(
                            out=stb[:, o + 512 * c:o + 512 * (c + 1)],
                            in0=zt2[:, ss], in1=scbc[:, cs], op=OP.mult),
                          reads=["zt2", sb], writes=[f"stb{par}"],
                          tag=f"su_{wname}{m}d{c}")
                if is_w1:
                    E("q_gpsimd", lambda e, o=o: e.dma_start(
                        out=scr[m // 2, :, 1024 * (m % 2):1024 * (m % 2 + 1)],
                        in_=stb[:, o:o + 1024]),
                      reads=[f"stb{par}"], writes=[f"{wname}@{par}"],
                      tag=f"su_{wname}{m}o")
                else:
                    for c in range(2):
                        E("q_gpsimd", lambda e, o=o, c=c: e.dma_start(
                            out=scr[c, m // 2, :, 512 * (m % 2):512 * (m % 2) + 512],
                            in_=stb[:, o + 512 * c:o + 512 * (c + 1)]),
                          reads=[f"stb{par}"], writes=[f"{wname}@{par}"],
                          tag=f"su_{wname}{m}o{c}")
            qh_s = s2 if not is_w1 else None
            return emit_loads, emit_rest

        for m in range(FT):
            quant_fillers.append(make_quant("s_w1", q1, s_w1, True, m))
        for m in range(FT):
            quant_fillers.append(make_quant("s_w2", q2, s_w2, False, m))


        def bbuf(slot):
            return f"bg{slot}"

        def ln_tile(tag, j, srcten, srcbuf, sg, dumpten, dumpbuf):
            ssum, sm, snm, svs = (stats[sg + "sum"], stats[sg + "m"],
                                  stats[sg + "nm"], stats[sg + "vs"])
            bsum, bm, bnm, bvs = (f"st{sg}sum", f"st{sg}m", f"st{sg}nm",
                                  f"st{sg}vs")
            E("vector", lambda e, j=j: e.reduce_sum(
                out=ssum[:, j:j + 1], in_=srcten[:, 1024 * j:1024 * (j + 1)],
                axis=AX.X), reads=[srcbuf], writes=[bsum], tag=f"{tag}_s{j}")
            E("vector", lambda e, j=j: e.tensor_scalar_mul(
                out=snm[:, j:j + 1], in0=ssum[:, j:j + 1], scalar1=-1.0 / D),
              sreads=[bsum], writes=[bnm], tag=f"{tag}_nm{j}")
            E("vector", lambda e, j=j: e.tensor_scalar_mul(
                out=sm[:, j:j + 1], in0=ssum[:, j:j + 1], scalar1=1.0 / D),
              sreads=[bsum], writes=[bm], tag=f"{tag}_m{j}")
            E("scalar", lambda e, j=j: e.activation(
                out=dumpten[:, 1024 * j:1024 * (j + 1)],
                in_=srcten[:, 1024 * j:1024 * (j + 1)], func=AF.Square,
                bias=snm[:, j:j + 1], accum_out=svs[:, j:j + 1]),
              reads=[srcbuf], sreads=[bnm], writes=[dumpbuf, bvs],
              tag=f"{tag}_q{j}")

        def ln_fin(tag, srcten, srcbuf, dstten, dstbuf, sg, post_cb=None):
            sm, svs, srs = stats[sg + "m"], stats[sg + "vs"], stats[sg + "rs"]
            bm, bvs, brs = f"st{sg}m", f"st{sg}vs", f"st{sg}rs"
            E("scalar", lambda e: e.activation(
                out=srs[:], in_=svs[:], func=AF.Ln, scale=1.0 / D, bias=ce5[:]),
              sreads=[bvs, "ce5"], writes=[brs], tag=f"{tag}_l")
            E("scalar", lambda e: e.activation(
                out=srs[:], in_=srs[:], func=AF.Exp, scale=-0.5),
              sreads=[brs], writes=[brs], tag=f"{tag}_e")
            for j in range(TT):
                E("vector", lambda e, j=j: e.tensor_scalar(
                    out=dstten[:, 1024 * j:1024 * (j + 1)],
                    in0=srcten[:, 1024 * j:1024 * (j + 1)],
                    scalar1=sm[:, j:j + 1], scalar2=srs[:, j:j + 1],
                    op0=OP.subtract, op1=OP.mult),
                  reads=[srcbuf], sreads=[bm, brs], writes=[dstbuf],
                  tag=f"{tag}_z{j}")
                if post_cb is not None:
                    post_cb(j)

        def emit_ln(tag, srcten, srcbuf, dstten, dstbuf, sg, dumpten, dumpbuf,
                    post_cb=None):
            for j in range(TT):
                ln_tile(tag, j, srcten, srcbuf, sg, dumpten, dumpbuf)
            ln_fin(tag, srcten, srcbuf, dstten, dstbuf, sg, post_cb)

        def emit_transpose(tag, srcten, srcbuf, dstten, dstbuf, gv, bv, gname, bname):
            for ci in range(KT):
                pst = psT0 if ci % 2 == 0 else psT1
                pbn = f"psT{ci % 2}"
                for j in range(TT):
                    E("tensor", lambda e, ci=ci, j=j, pst=pst: e.transpose(
                        pst[:, 128 * j:128 * (j + 1)],
                        srcten[:, 1024 * j + 128 * ci:1024 * j + 128 * (ci + 1)],
                        ident_sb[:]),
                      reads=[srcbuf, "ident_sb"], writes=[pbn], tag=f"{tag}_t{ci}_{j}")
                E("scalar", lambda e, ci=ci, pst=pst: e.activation(
                    out=dstten[:, 512 * ci:512 * (ci + 1)], in_=pst[:],
                    func=AF.Identity, scale=gv[:, ci:ci + 1], bias=bv[:, ci:ci + 1]),
                  reads=[pbn], sreads=[gname, bname], writes=[f"{dstbuf}{ci}"],
                  tag=f"{tag}_e{ci}")

        def stage_pre(h):
            hh = f"h{h}"
            E("q_sync", lambda e, h=h: e.dma_start(
                out=x0[:].rearrange("p (j n) -> p j n", j=4),
                in_=xcore[HALF * h:HALF * (h + 1)].rearrange(
                    "(j p) n -> p j n", p=128)),
              writes=["x0"], tag=f"{hh}_x0")
            emit_ln(f"{hh}_ln1", x0, "x0", ztm, "ztm", "A", ztm, "ztm")

        def stage_tr1(h):
            emit_transpose(f"h{h}_tr1", ztm, "ztm", TA, "TA", vg1, vb1,
                           "vg1", "vb1")

        def mix_one(h, vm, vmn, slot, upd_carry):
            hh = f"h{h}"
            for ci in range(KT):
                E("vector", lambda e, ci=ci: e.tensor_tensor(
                    out=sc_bb[:, 1:512], in0=TA[:, 512 * ci + 1:512 * (ci + 1)],
                    in1=TA[:, 512 * ci:512 * (ci + 1) - 1], op=OP.subtract),
                  reads=[f"TA{ci}"], writes=["bb0"], tag=f"{hh}_d{slot}_{ci}")
                E("vector", lambda e, ci=ci: e.tensor_tensor(
                    out=sc_bb[:, 0:1], in0=TA[:, 512 * ci:512 * ci + 1],
                    in1=vcx[:, ci:ci + 1], op=OP.subtract),
                  reads=[f"TA{ci}"], sreads=["vcx"], writes=["bb0"],
                  tag=f"{hh}_d0{slot}_{ci}")
                E("vector", lambda e, ci=ci, vm=vm: e.scalar_tensor_tensor(
                    out=bigbf[:, 512 * (slot + ci) + 1:512 * (slot + ci + 1)],
                    in0=sc_bb[:, 1:512], scalar=vm[:, ci:ci + 1],
                    in1=TA[:, 512 * ci:512 * (ci + 1) - 1],
                    op0=OP.mult, op1=OP.add),
                  reads=["bb0", f"TA{ci}"], sreads=[vmn],
                  writes=[bbuf(slot + ci)], tag=f"{hh}_mx{slot}_{ci}")
                E("vector", lambda e, ci=ci, vm=vm: e.scalar_tensor_tensor(
                    out=bigbf[:, 512 * (slot + ci):512 * (slot + ci) + 1],
                    in0=sc_bb[:, 0:1], scalar=vm[:, ci:ci + 1],
                    in1=vcx[:, ci:ci + 1], op0=OP.mult, op1=OP.add),
                  reads=["bb0"], sreads=[vmn, "vcx"], writes=[bbuf(slot + ci)],
                  tag=f"{hh}_mx0{slot}_{ci}")
                if upd_carry:
                    E("vector", lambda e, ci=ci: e.tensor_copy(
                        out=vcx[:, ci:ci + 1],
                        in_=TA[:, 512 * (ci + 1) - 1:512 * (ci + 1)]),
                      sreads=[f"TA{ci}"], writes=["vcx"], tag=f"{hh}_cx{ci}")

        def stage_mix_k(h):
            mix_one(h, vmk, "vmk", 0, False)

        def stage_mix_vr(h):
            mix_one(h, vmv, "vmv", 8, False)
            mix_one(h, vmr, "vmr", 16, True)

        KVPS = [(psA, "psA"), (psB, "psB"), (psO[0], "psO0"), (psO[1], "psO1")]
        RPS = [(psO[2], "psO2"), (psO[3], "psO3")]

        def kvr_mm(hh, mo, wlo, wlo_n, whi, whi_n, slot, tag, pool=None):
            ps, psn = (pool or KVPS)[mo % len(pool or KVPS)]
            for k in range(KT):
                wt, wn = (wlo, wlo_n) if k < 4 else (whi, whi_n)
                E("tensor", lambda e, mo=mo, k=k, ps=ps, wt=wt: e.matmul(
                    ps[:],
                    wt[:, 1024 * (k % 4) + 128 * mo:1024 * (k % 4) + 128 * (mo + 1)],
                    bigbf[:, 512 * (slot + k):512 * (slot + k + 1)],
                    start=(k == 0), stop=(k == KT - 1)),
                  reads=[wn, bbuf(slot + k)], writes=[psn], tag=f"{tag}{mo}_{k}")
            return ps, psn

        def stage_attn(h):
            hh = f"h{h}"
            E("q_sync", lambda e: e.dma_start(out=wsm[0][:], in_=s_wk[0]),
              reads=["s_wk@0", "s_wk@1"], writes=["wsm0"], tag=f"{hh}_ldk0")
            E("q_sync", lambda e: e.dma_start(out=wsm[1][:], in_=s_wk[1]),
              reads=["s_wk@0", "s_wk@1"], writes=["wsm1"], tag=f"{hh}_ldk1")
            E("q_sync", lambda e: e.dma_start(out=wsm[2][:], in_=s_wv[0]),
              reads=["s_wv@0", "s_wv@1"], writes=["wsm2"], tag=f"{hh}_ldv0")
            for mo in range(KT):
                ps, psn = kvr_mm(hh, mo, wsm[0], "wsm0", wsm[1], "wsm1", 0,
                                 f"{hh}_kmm")
                E("scalar", lambda e, mo=mo, ps=ps: e.activation(
                    out=T1[:, 512 * mo:512 * (mo + 1)], in_=ps[:], func=AF.Exp),
                  reads=[psn], writes=["T1"], tag=f"{hh}_ek{mo}")
                if h == 0:
                    E("scalar", lambda e, mo=mo, ps=ps: e.activation(
                        out=T1[:, 512 * mo:512 * mo + 1], in_=ps[:, 0:1],
                        func=AF.Exp, bias=vtf[:, mo:mo + 1]),
                      reads=[psn], sreads=["vtf"], writes=["T1"],
                      tag=f"{hh}_ektf{mo}")
                    pump_pair()
                    pump_pair()
                    pump_pair()
            E("q_sync", lambda e: e.dma_start(out=wsm[0][:], in_=s_wv[1]),
              reads=["s_wv@0", "s_wv@1"], writes=["wsm0"], tag=f"{hh}_ldv1")
            E("q_sync", lambda e: e.dma_start(out=wsm[1][:], in_=s_wr[0]),
              reads=["s_wr@0", "s_wr@1"], writes=["wsm1"], tag=f"{hh}_ldr0")
            for mo in range(KT):
                ps, psn = kvr_mm(hh, mo, wsm[2], "wsm2", wsm[0], "wsm0", 8,
                                 f"{hh}_vmm")
                E("vector", lambda e, mo=mo, ps=ps: e.tensor_tensor(
                    out=sc_ekv[:], in0=T1[:, 512 * mo:512 * (mo + 1)], in1=ps[:],
                    op=OP.mult),
                  reads=["T1", psn], writes=["ekv"], tag=f"{hh}_ekv{mo}")
                sl_e = mo % 2
                ese = slice(512 * sl_e, 512 * (sl_e + 1))
                E("scalar", lambda e, mo=mo, ese=ese: e.activation(
                    out=ew_t[:, ese], in_=ew_t[:, ese], func=AF.Exp, scale=0.0,
                    bias=vtd[:, mo:mo + 1]),
                  reads=[f"ewt{sl_e}"], sreads=["vtd"], writes=[f"ewt{sl_e}"],
                  tag=f"{hh}_ew{mo}")
                E("vector", lambda e, mo=mo, ese=ese: e.tensor_tensor_scan(
                    out=sc_aa[:], data0=ew_t[:, ese],
                    data1=sc_ekv[:], initial=vca[:, mo:mo + 1],
                    op0=OP.mult, op1=OP.add),
                  reads=[f"ewt{sl_e}", "ekv"], sreads=["vca"], writes=["aa"],
                  tag=f"{hh}_sa{mo}")
                E("vector", lambda e, mo=mo: e.tensor_copy(
                    out=vca[:, mo:mo + 1], in_=sc_aa[:, 511:512]),
                  sreads=["aa"], writes=["vca"], tag=f"{hh}_ca{mo}")
                bo = 512 * (mo % 2)
                bn = f"bb{mo % 2}"
                E("vector", lambda e, mo=mo, ese=ese, bo=bo: e.tensor_tensor_scan(
                    out=sc_bb[:, bo:bo + 512], data0=ew_t[:, ese],
                    data1=T1[:, 512 * mo:512 * (mo + 1)],
                    initial=vcb[:, mo:mo + 1], op0=OP.mult, op1=OP.add),
                  reads=[f"ewt{sl_e}", "T1"], sreads=["vcb"], writes=[bn],
                  tag=f"{hh}_sb{mo}")
                E("vector", lambda e, mo=mo, bo=bo: e.tensor_copy(
                    out=vcb[:, mo:mo + 1], in_=sc_bb[:, bo + 511:bo + 512]),
                  sreads=[bn], writes=["vcb"], tag=f"{hh}_cb{mo}")
                E("scalar", lambda e, bo=bo: e.activation(
                    out=sc_bb[:, bo:bo + 512], in_=sc_bb[:, bo:bo + 512],
                    func=AF.Ln, bias=ce8[:]),
                  sreads=[bn, "ce8"], writes=[bn], tag=f"{hh}_lb{mo}")
                E("scalar", lambda e, bo=bo: e.activation(
                    out=sc_bb[:, bo:bo + 512], in_=sc_bb[:, bo:bo + 512],
                    func=AF.Exp, scale=-1.0),
                  sreads=[bn], writes=[bn], tag=f"{hh}_eb{mo}")
                E("vector", lambda e, mo=mo, bo=bo: e.tensor_tensor(
                    out=T2[:, 512 * mo:512 * (mo + 1)], in0=sc_aa[:],
                    in1=sc_bb[:, bo:bo + 512], op=OP.mult),
                  reads=["aa", bn], writes=["T2"], tag=f"{hh}_wk{mo}")
                if h == 0:
                    pump_pair("scalar", "scalar")
            E("q_sync", lambda e: e.dma_start(out=wsm[2][:], in_=s_wr[1]),
              reads=["s_wr@0", "s_wr@1"], writes=["wsm2"], tag=f"{hh}_ldr1")
            E("q_sync", lambda e: e.dma_start(out=wsm[0][:], in_=s_wo[0]),
              reads=["s_wo@0", "s_wo@1"], writes=["wsm0"], tag=f"{hh}_ldo0")
            sig_fns = []
            for mo in range(KT):
                ps, psn = kvr_mm(hh, mo, wsm[1], "wsm1", wsm[2], "wsm2", 16,
                                 f"{hh}_rmm", pool=RPS)

                def mkf(mo=mo, ps=ps, psn=psn):
                    E("scalar", lambda e: e.activation(
                        out=rsg[:], in_=ps[:], func=AF.Sigmoid),
                      reads=[psn], writes=["rsg"], tag=f"{hh}_sg{mo}")
                    E("vector", lambda e: e.tensor_tensor(
                        out=bigbf[:, 512 * (24 + mo):512 * (25 + mo)],
                        in0=rsg[:], in1=T2[:, 512 * mo:512 * (mo + 1)],
                        op=OP.mult),
                      reads=["rsg", "T2"], writes=[bbuf(24 + mo)],
                      tag=f"{hh}_gt{mo}")
                sig_fns.append(mkf)
                if mo > 0:
                    sig_fns[mo - 1]()
                if h == 0:
                    pump_pair()
                    pump_pair()
            sig_fns[KT - 1]()
            E("q_sync", lambda e: e.dma_start(out=wsm[1][:], in_=s_wo[1]),
              reads=["s_wo@0", "s_wo@1"], writes=["wsm1"], tag=f"{hh}_ldo1")
            E("q_sync", lambda e: e.dma_start(out=wsm[2][:], in_=s_wfr[0]),
              reads=["s_wfr@0", "s_wfr@1"], writes=["wsm2"], tag=f"{hh}_ldf0")
            for j in range(TT):
                p0, p1 = psO[2 * (j % 2)], psO[2 * (j % 2) + 1]
                p0n, p1n = f"psO{2 * (j % 2)}", f"psO{2 * (j % 2) + 1}"
                for k in range(KT):
                    wt, wn = (wsm[0], "wsm0") if k < 4 else (wsm[1], "wsm1")
                    gsl = slice(512 * (24 + k) + 128 * j,
                                512 * (24 + k) + 128 * (j + 1))
                    E("tensor", lambda e, p0=p0, wt=wt, k=k, gsl=gsl: e.matmul(
                        p0[:], bigbf[:, gsl],
                        wt[:, 1024 * (k % 4):1024 * (k % 4) + 512],
                        start=(k == 0), stop=(k == KT - 1)),
                      reads=[bbuf(24 + k), wn], writes=[p0n],
                      tag=f"{hh}_wo{j}_{k}a")
                    E("tensor", lambda e, p1=p1, wt=wt, k=k, gsl=gsl: e.matmul(
                        p1[:], bigbf[:, gsl],
                        wt[:, 1024 * (k % 4) + 512:1024 * (k % 4) + 1024],
                        start=(k == 0), stop=(k == KT - 1)),
                      reads=[bbuf(24 + k), wn], writes=[p1n],
                      tag=f"{hh}_wo{j}_{k}b")
                for hi, pp, ppn in [(0, p0, p0n), (1, p1, p1n)]:
                    off = 1024 * j + 512 * hi
                    E("vector", lambda e, pp=pp, off=off: e.tensor_tensor(
                        out=Q[:, off:off + 512], in0=x0[:, off:off + 512],
                        in1=pp[:], op=OP.add),
                      reads=["x0", ppn], writes=["Q"], tag=f"{hh}_u{j}{hi}")
                    E("vector", lambda e, pp=pp, off=off: e.scalar_tensor_tensor(
                        out=T2[:, off:off + 512], in0=x0[:, off:off + 512],
                        scalar=2.0, in1=pp[:], op0=OP.mult, op1=OP.add),
                      reads=["x0", ppn], writes=["T2"], tag=f"{hh}_s01{j}{hi}")
                ln_tile(f"{hh}_ln2", j, Q, "Q", "B", ztm, "ztm")
                if h == 0:
                    pump_pair()
            E("q_sync", lambda e: e.dma_start(out=wsm[0][:], in_=s_wfr[1]),
              reads=["s_wfr@0", "s_wfr@1"], writes=["wsm0"], tag=f"{hh}_ldf1")
            ln_fin(f"{hh}_ln2", Q, "Q", ztm, "ztm", "B")
            emit_transpose(f"{hh}_tr2", ztm, "ztm", TA, "TA", vg2, vb2,
                           "vg2", "vb2")
            for ci in range(KT):
                E("vector", lambda e, ci=ci: e.tensor_tensor(
                    out=TB[:, 512 * ci + 1:512 * (ci + 1)],
                    in0=TA[:, 512 * ci + 1:512 * (ci + 1)],
                    in1=TA[:, 512 * ci:512 * (ci + 1) - 1], op=OP.add),
                  reads=[f"TA{ci}"], writes=[f"TB{ci}"], tag=f"{hh}_ym{ci}")
                E("vector", lambda e, ci=ci: e.tensor_tensor(
                    out=TB[:, 512 * ci:512 * ci + 1],
                    in0=TA[:, 512 * ci:512 * ci + 1], in1=vcy[:, ci:ci + 1],
                    op=OP.add),
                  reads=[f"TA{ci}"], sreads=["vcy"], writes=[f"TB{ci}"],
                  tag=f"{hh}_ym0{ci}")
                E("vector", lambda e, ci=ci: e.tensor_copy(
                    out=vcy[:, ci:ci + 1],
                    in_=TA[:, 512 * (ci + 1) - 1:512 * (ci + 1)]),
                  sreads=[f"TA{ci}"], writes=["vcy"], tag=f"{hh}_cy{ci}")
            for j in range(TT):
                p0, p1 = psO[2 * (j % 2)], psO[2 * (j % 2) + 1]
                p0n, p1n = f"psO{2 * (j % 2)}", f"psO{2 * (j % 2) + 1}"
                for k in range(KT):
                    wt, wn = (wsm[2], "wsm2") if k < 4 else (wsm[0], "wsm0")
                    asl = slice(512 * k + 128 * j, 512 * k + 128 * (j + 1))
                    E("tensor", lambda e, p0=p0, wt=wt, k=k, asl=asl: e.matmul(
                        p0[:], TA[:, asl],
                        wt[:, 1024 * (k % 4):1024 * (k % 4) + 512],
                        start=(k == 0), stop=(k == KT - 1)),
                      reads=[f"TA{k}", wn], writes=[p0n],
                      tag=f"{hh}_fr{j}_{k}a")
                    E("tensor", lambda e, p1=p1, wt=wt, k=k, asl=asl: e.matmul(
                        p1[:], TA[:, asl],
                        wt[:, 1024 * (k % 4) + 512:1024 * (k % 4) + 1024],
                        start=(k == 0), stop=(k == KT - 1)),
                      reads=[f"TA{k}", wn], writes=[p1n],
                      tag=f"{hh}_fr{j}_{k}b")
                for hi, pp, ppn in [(0, p0, p0n), (1, p1, p1n)]:
                    off = 1024 * j + 512 * hi
                    E("scalar", lambda e, pp=pp, off=off: e.activation(
                        out=T1[:, off:off + 512], in_=pp[:], func=AF.Sigmoid,
                        scale=2.0),
                      reads=[ppn], writes=["T1"], tag=f"{hh}_rr{j}{hi}")

        def emit_mm2(tagp, mp, cho, bias_close):
            mm, mpar = mp
            sl = (mm // 2) % 2
            last = (mm == FT - 1)
            for j in range(TT):
                E("tensor", lambda e, mm=mm, j=j, mpar=mpar, sl=sl: e.matmul(
                    psO[j][:],
                    bigbf[:, 512 * mm + 128 * j:512 * mm + 128 * (j + 1)],
                    fvb[:, 1024 * sl + 512 * mpar:1024 * sl + 512 * (mpar + 1)],
                    start=(mm == 0), stop=(last and not bias_close)),
                  reads=[bbuf(mm), f"fvb{2 * sl + mpar}"], writes=[f"psO{j}"],
                  tag=f"{tagp}m2_{cho}_{mm}_{j}")

        def ffn_pair(tagp, blkn, rown, scr_blk, scr_row, evac_fn, comb_fn,
                     bias_close, fillers=None):
            for cho in range(2):
                pend = []
                for m in range(FT):
                    par = m % 2
                    sl = (m // 2) % 2
                    if cho == 0:
                        E("q_sync", lambda e, m=m, par=par: e.dma_start(
                            out=fkb[:, 1024 * par:1024 * (par + 1)],
                            in_=scr_blk[m // 2, :,
                                        1024 * (m % 2):1024 * (m % 2 + 1)]),
                          reads=[blkn + "@0", blkn + "@1"],
                          writes=[f"fkb{par}"], tag=f"{tagp}b{m}")
                    if par == 0:
                        E("q_sync", lambda e, m=m, cho=cho, sl=sl: e.dma_start(
                            out=fvb[:, 1024 * sl:1024 * (sl + 1)],
                            in_=scr_row[cho, m // 2]),
                          reads=[rown + "@0", rown + "@1"],
                          writes=[f"fvb{2 * sl}", f"fvb{2 * sl + 1}"],
                          tag=f"{tagp}r{m}_{cho}")
                    if cho == 0:
                        ps, psn = (psA, "psA") if par == 0 else (psB, "psB")
                        for k in range(KT):
                            E("tensor", lambda e, m=m, k=k, ps=ps, par=par: e.matmul(
                                ps[:],
                                fkb[:, 1024 * par + 128 * k:1024 * par + 128 * (k + 1)],
                                TB[:, 512 * k:512 * (k + 1)],
                                start=(k == 0), stop=(k == KT - 1)),
                              reads=[f"fkb{par}", f"TB{k}"], writes=[psn],
                              tag=f"{tagp}m1_{m}_{k}")
                        evac_fn(m, ps, psn, par)
                    if fillers:
                        for f in fillers(m, cho):
                            f()
                    pend.append((m, par))
                    if len(pend) >= 2:
                        emit_mm2(tagp, pend.pop(0), cho, bias_close)
                while pend:
                    emit_mm2(tagp, pend.pop(0), cho, bias_close)
                comb_fn(cho)

        def stage_fA(h, fillers=None):
            hh = f"h{h}"

            def kk_evac(m, ps, psn, par):
                E("scalar", lambda e, ps=ps, m=m: e.activation(
                    out=bigbf[:, 512 * m:512 * (m + 1)], in_=ps[:],
                    func=AF.Relu),
                  reads=[psn], writes=[bbuf(m)], tag=f"{hh}_re{m}")
                E("vector", lambda e, m=m: e.tensor_tensor(
                    out=bigbf[:, 512 * m:512 * (m + 1)],
                    in0=bigbf[:, 512 * m:512 * (m + 1)],
                    in1=bigbf[:, 512 * m:512 * (m + 1)], op=OP.mult),
                  reads=[bbuf(m)], writes=[bbuf(m)], tag=f"{hh}_sq{m}")

            def comb1(cho):
                for j in range(TT):
                    off = 1024 * j + 512 * cho
                    E("vector", lambda e, j=j, off=off: e.tensor_tensor(
                        out=sc_ekv[:], in0=T1[:, off:off + 512], in1=psO[j][:],
                        op=OP.mult),
                      reads=["T1", f"psO{j}"], writes=["ekv"],
                      tag=f"{hh}_g1{cho}{j}")
                    E("vector", lambda e, off=off: e.tensor_tensor(
                        out=T2[:, off:off + 512], in0=T2[:, off:off + 512],
                        in1=sc_ekv[:], op=OP.add),
                      reads=["ekv", "T2"], writes=["T2"], tag=f"{hh}_a1{cho}{j}")
                    if cho == 1:
                        ln_tile(f"{hh}_pl1", j, T2, "T2", "B", zt2, "zt2")

            ffn_pair(f"{hh}_fA", "s_fk", "s_fv", s_fk, s_fv, kk_evac, comb1,
                     False, fillers)

        def stage_mid(h):
            hh = f"h{h}"
            svs = stats["Bvs"]
            sS = stats["Ars"]
            sm = stats["Bm"]
            E("scalar", lambda e: e.activation(
                out=sS[:], in_=svs[:], func=AF.Ln, scale=(1.0 + EPS5) / D,
                bias=ce52[:]),
              sreads=["stBvs", "ce52"], writes=["stArs"], tag=f"{hh}_r5")
            E("scalar", lambda e: e.activation(
                out=sS[:], in_=sS[:], func=AF.Exp, scale=-0.5),
              sreads=["stArs"], writes=["stArs"], tag=f"{hh}_r6")
            for j in range(TT):
                E("vector", lambda e, j=j: e.tensor_scalar(
                    out=zt2[:, 1024 * j:1024 * (j + 1)],
                    in0=T2[:, 1024 * j:1024 * (j + 1)],
                    scalar1=sm[:, j:j + 1], scalar2=sS[:, j:j + 1],
                    op0=OP.subtract, op1=OP.mult),
                  reads=["T2"], sreads=["stBm", "stArs"], writes=["zt2"],
                  tag=f"{hh}_z3{j}")
            emit_transpose(f"{hh}_tr3", zt2, "zt2", TB, "TB", vg3, vb3,
                           "vg3", "vb3")
            ln_fin(f"{hh}_pl1", T2, "T2", Q, "Q", "B")

        def stage_fB(h):
            hh = f"h{h}"

            def silu_evac(m, ps, psn, par):
                E("scalar", lambda e, m=m, ps=ps: e.activation(
                    out=bigbf[:, 512 * m:512 * (m + 1)], in_=ps[:], func=AF.Silu,
                    bias=vfb1[:, m:m + 1]),
                  reads=[psn], sreads=["vfb1"], writes=[bbuf(m)],
                  tag=f"{hh}_si{m}")

            def comb2(cho):
                for j in range(TT):
                    E("tensor", lambda e, j=j, cho=cho: e.matmul(
                        psO[j][:], ones1[:],
                        b2row_b[0:1, 512 * cho:512 * (cho + 1)],
                        start=False, stop=True),
                      reads=["ones1", "b2row_b"], writes=[f"psO{j}"],
                      tag=f"{hh}_bz{cho}{j}")
                    off = 1024 * j + 512 * cho
                    E("vector", lambda e, j=j, off=off: e.scalar_tensor_tensor(
                        out=T2[:, off:off + 512], in0=Q[:, off:off + 512],
                        scalar=2.0, in1=psO[j][:], op0=OP.mult, op1=OP.add),
                      reads=["Q", f"psO{j}"], writes=["T2"],
                      tag=f"{hh}_c2{cho}{j}")

            ffn_pair(f"{hh}_fB", "s_w1", "s_w2", s_w1, s_w2, silu_evac, comb2,
                     True, fillers=mk_fB_fillers(h))

        def stage_tail(h):
            hh = f"h{h}"
            for j in range(TT):
                ln_tile(f"{hh}_pl2", j, T2, "T2", "B", zt2, "zt2")
            ln_fin(f"{hh}_pl2", T2, "T2", Q, "Q", "B")
            E("q_gpsimd", lambda e, h=h: e.dma_start(
                out=out_h[HALF * h:HALF * (h + 1)].rearrange(
                    "(j p) n -> p j n", p=128),
                in_=Q[:].rearrange("p (j n) -> p j n", j=4)),
              reads=["Q"], writes=[f"od{h}"], tag=f"{hh}_out")

        def mk_fillers(h):
            def fillers(m, cho):
                out = []
                if cho != 0:
                    return out
                if h == 0:
                    for qi in (2 * m, 2 * m + 1):
                        if qi < len(quant_fillers):
                            out.append(quant_fillers[qi][1])
                    for qi in (2 * m + 2, 2 * m + 3):
                        if qi < len(quant_fillers):
                            out.append(quant_fillers[qi][0])
                if m == 4 and h + 1 < NH:
                    out.append(lambda: stage_pre(h + 1))
                return out
            return fillers

        def mk_fB_fillers(h):
            def fillers(m, cho):
                if cho == 1 and m == 10 and h + 1 < NH:
                    return [lambda: stage_mix_k(h + 1)]
                return []
            return fillers

        pump_pair()
        pump_pair()
        stage_pre(0)
        pump_pair()
        pump_pair()
        stage_tr1(0)
        pump_pair()
        pump_pair()
        stage_mix_k(0)
        pump_pair()
        pump_pair()
        pump_pair()
        pump_pair()
        for h in range(NH):
            stage_mix_vr(h)
            stage_attn(h)
            while cast_jobs:
                pump_pair()
            if h == 0:
                quant_fillers[0][0]()
                quant_fillers[1][0]()
            stage_fA(h, fillers=mk_fillers(h))
            if h + 1 < NH:
                stage_tr1(h + 1)
            stage_mid(h)
            stage_fB(h)
            stage_tail(h)

        E("gpsimd", lambda e: None, reads=[f"od{h}" for h in range(NH)],
          tag="final_wait")
        S.build()
        sems = {}
        for key in sorted({it["skey"] for it in S.items}):
            sems[key] = st.enter_context(nc.semaphore("sem_" + key))
        S.sems = sems
        block = st.enter_context(nc.Block())
        S.materialize(block)
    return nc


_NC_CACHE = None


def kernel(**inputs):
    global _NC_CACHE
    x = np.ascontiguousarray(inputs["L0_attn_x"])
    if _NC_CACHE is None:
        _NC_CACHE = build_program()
    nc = _NC_CACHE
    ident = np.eye(128, dtype=ml_dtypes.bfloat16)
    base = {k: np.ascontiguousarray(v) for k, v in inputs.items()
            if k != "L0_attn_x"}
    in_maps = []
    for i in range(8):
        m = dict(base)
        m["xcore"] = np.ascontiguousarray(x[i])
        m["ident"] = ident
        in_maps.append(m)
    res = run_bass_kernel_spmd(nc, in_maps, core_ids=list(range(8)))
    out = np.stack([res.results[i]["out"] for i in range(8)], axis=0)
    return out.astype(np.float32)


# version 27
# speedup vs baseline: 1.0372x; 1.0191x over previous
import contextlib
import numpy as np
import ml_dtypes
import concourse.bass as bass
import concourse.mybir as mybir
from concourse.bass_utils import run_bass_kernel_spmd

F32 = mybir.dt.float32
BF16 = mybir.dt.bfloat16
I32 = mybir.dt.int32
AF = mybir.ActivationFunctionType
OP = mybir.AluOpType
AX = mybir.AxisListType

D = 1024
KT = 8
HALF = 512
NH = 8
TT = 4
FF = 4096
FT = 32
EPS5 = 1e-5
EPS8 = 1e-8

COMPUTE_ENGINES = ("tensor", "vector", "scalar", "gpsimd")
DMA_QUEUES = {"q_sync": "sync", "q_gpsimd": "gpsimd", "q_scalar": "scalar",
              "q_vector": "vector"}


class Buf:
    __slots__ = ("name", "last_write", "readers")

    def __init__(self, name):
        self.name = name
        self.last_write = None
        self.readers = []


class Sched:

    def __init__(self, nc, dma_group=None, total_groups=()):
        self.nc = nc
        self.items = []
        self._bufs = {}
        self.sems = None
        self.dma_group = dma_group or (lambda buf: buf)
        self.total_groups = set(total_groups)

    def buf(self, name):
        b = self._bufs.get(name)
        if b is None:
            b = self._bufs[name] = Buf(name)
        return b

    def emit(self, stream, fn, reads=(), writes=(), sreads=(), tag=""):
        idx = len(self.items)
        reads = [self.buf(r) if isinstance(r, str) else r for r in reads]
        sreads = [self.buf(r) if isinstance(r, str) else r for r in sreads]
        writes = [self.buf(w) if isinstance(w, str) else w for w in writes]
        rdeps = set()
        wdeps = set()
        hard = set()
        for b in reads:
            if b.last_write is not None:
                rdeps.add(b.last_write)
        for b in sreads:
            if b.last_write is not None:
                hard.add(b.last_write)
        for b in writes:
            if b.last_write is not None:
                wdeps.add(b.last_write)
            for r in b.readers:
                wdeps.add(r)
        if stream in DMA_QUEUES:
            assert writes, f"DMA item {tag} needs a write buf for its semaphore"
            skey = "d_" + self.dma_group(writes[0].name)
        else:
            skey = stream
        self.items.append(dict(idx=idx, stream=stream, fn=fn, rdeps=rdeps,
                               wdeps=wdeps, hard=hard, tag=tag, signals=False,
                               skey=skey, sem_val=None, waits=None))
        for b in list(reads) + list(sreads):
            b.readers.append(idx)
        for b in writes:
            b.last_write = idx
            b.readers = []
        return idx

    def build(self):
        items = self.items
        for it in items:
            if it["stream"] in DMA_QUEUES:
                it["signals"] = True
            for d in list(it["rdeps"]) + list(it["wdeps"]):
                dep = items[d]
                if dep["stream"] == "tensor" and it["stream"] == "tensor":
                    continue
                dep["signals"] = True
            for d in it["hard"]:
                items[d]["signals"] = True
        counters = {}
        for it in items:
            if it["signals"]:
                step = 16 if it["stream"] in DMA_QUEUES else 1
                counters[it["skey"]] = counters.get(it["skey"], 0) + step
                it["sem_val"] = counters[it["skey"]]
        self.sem_totals = dict(counters)
        for it in items:
            need = {}

            def add(dep, force_total):
                s, v = dep["skey"], dep["sem_val"]
                assert v is not None
                if force_total and s.startswith("d_") and s[2:] in self.total_groups:
                    v = self.sem_totals[s]
                if need.get(s, 0) < v:
                    need[s] = v
            for d in it["rdeps"]:
                dep = items[d]
                if dep["stream"] == "tensor" and it["stream"] == "tensor":
                    continue
                add(dep, True)
            for d in it["wdeps"]:
                dep = items[d]
                if dep["stream"] == "tensor" and it["stream"] == "tensor":
                    continue
                if (dep["skey"] == it["skey"] and it["stream"] in DMA_QUEUES
                        and it["skey"].startswith("d_")
                        and it["skey"][2:] in self.total_groups):
                    continue
                add(dep, False)
            for d in it["hard"]:
                add(items[d], True)
            it["waits"] = need
        seen = {}
        for it in items:
            ee = DMA_QUEUES.get(it["stream"], it["stream"])
            got = seen.setdefault(ee, {})
            w = {}
            for s, v in sorted(it["waits"].items()):
                if got.get(s, 0) < v:
                    w[s] = v
                    got[s] = v
            it["waits"] = w

    def materialize(self, block):
        items = self.items
        sems = self.sems

        def run_stream(eng_name, eng):
            for it in items:
                if DMA_QUEUES.get(it["stream"], it["stream"]) != eng_name:
                    continue
                for s, v in sorted(it["waits"].items()):
                    eng.wait_ge(sems[s], v)
                inst = it["fn"](eng)
                if it["signals"]:
                    step = 16 if it["stream"] in DMA_QUEUES else 1
                    assert inst is not None, f"item {it['idx']} {it['tag']}"
                    inst.then_inc(sems[it["skey"]], step)

        for name in list(COMPUTE_ENGINES) + ["sync"]:
            if any(DMA_QUEUES.get(it["stream"], it["stream"]) == name
                   for it in self.items):
                getattr(block, name)(lambda eng, n=name: run_stream(n, eng))


def build_program():
    nc = bass.Bass("TRN2", target_bir_lowering=False)
    dr = lambda n, s, d: nc.dram_tensor(n, s, d, kind="ExternalInput").ap()
    xcore = dr("xcore", [NH * HALF, D], F32)
    ident = dr("ident", [128, 128], BF16)
    Wk = dr("L0_attn_Wk", [D, D], F32)
    Wv = dr("L0_attn_Wv", [D, D], F32)
    Wr = dr("L0_attn_Wr", [D, D], F32)
    Wo = dr("L0_attn_Wo", [D, D], F32)
    Wfr = dr("L0_attn_Wfr", [D, D], F32)
    Wfk = dr("L0_attn_Wfk", [D, FF], F32)
    Wfv = dr("L0_attn_Wfv", [FF, D], F32)
    g1 = dr("L0_attn_ln1_g", [D], F32)
    b1 = dr("L0_attn_ln1_b", [D], F32)
    g2 = dr("L0_attn_ln2_g", [D], F32)
    b2 = dr("L0_attn_ln2_b", [D], F32)
    td = dr("L0_attn_time_decay", [D], F32)
    tf = dr("L0_attn_time_first", [D], F32)
    mk = dr("L0_attn_mix_k", [D], F32)
    mvv = dr("L0_attn_mix_v", [D], F32)
    mr = dr("L0_attn_mix_r", [D], F32)
    g3 = dr("L0_ffn_ln_gamma", [D], F32)
    b3 = dr("L0_ffn_ln_beta", [D], F32)
    q1 = dr("L0_ffn_fc1_q", [D, FF], I32)
    s1 = dr("L0_ffn_fc1_scales", [KT, FF], F32)
    fb1 = dr("L0_ffn_fc1_bias", [FF], F32)
    q2 = dr("L0_ffn_fc2_q", [FF, D], I32)
    s2 = dr("L0_ffn_fc2_scales", [FT, D], F32)
    fb2 = dr("L0_ffn_fc2_bias", [D], F32)
    out_h = nc.dram_tensor("out", [NH * HALF, D], F32, kind="ExternalOutput").ap()

    sc = lambda n, s: nc.dram_tensor(n, s, BF16, kind="Internal").ap()
    s_wk = sc("s_wk", [2, 128, 4096])
    s_wv = sc("s_wv", [2, 128, 4096])
    s_wr = sc("s_wr", [2, 128, 4096])
    s_wo = sc("s_wo", [2, 128, 4096])
    s_wfr = sc("s_wfr", [2, 128, 4096])
    s_fk = sc("s_fk", [16, 128, 2048])
    s_w1 = sc("s_w1", [16, 128, 2048])
    s_fv = sc("s_fv", [2, 16, 128, 1024])
    s_w2 = sc("s_w2", [2, 16, 128, 1024])

    TINY = {"vg1", "vb1", "vg2", "vb2", "vg3", "vb3", "vmk", "vmv", "vmr",
            "vtf", "vtd", "vfb1", "ident_sb"}

    def dma_group(buf):
        if buf in TINY:
            return "tv"
        if buf.endswith("@0"):
            return "swo0"
        if buf.endswith("@1"):
            return "swo1"
        return buf

    S = Sched(nc, dma_group=dma_group, total_groups=TINY | {"tv"})
    with contextlib.ExitStack() as st:
        T = lambda n, shp, d: st.enter_context(nc.sbuf_tensor(n, shp, d))
        x0 = T("x0", [128, 4096], F32)
        Q = T("Q", [128, 4096], F32)
        T1 = T("T1", [128, 4096], F32)
        T2 = T("T2", [128, 4096], F32)
        ztm = T("ztm", [128, 4096], BF16)
        zt2 = T("zt2", [128, 4096], BF16)
        TA = T("TA", [128, 4096], BF16)
        TB = T("TB", [128, 4096], BF16)
        bigbf = T("bigbf", [128, 16384], BF16)
        sc_ekv = T("sc_ekv", [128, 512], F32)
        sc_aa = T("sc_aa", [128, 512], F32)
        sc_bb = T("sc_bb", [128, 1024], F32)
        rsg = T("rsg", [128, 512], BF16)
        wsm = [T(f"wsm{i}", [128, 4096], BF16) for i in range(3)]
        fkb = T("fkb", [128, 2048], BF16)
        fvb = T("fvb", [128, 2048], BF16)
        ew_t = T("ew_t", [128, 1024], F32)
        stf = T("stf", [128, 3072], F32)
        stb = T("stb", [128, 2048], BF16)
        stq = T("stq", [128, 2048], I32)
        scbc = T("scbc", [128, 1024], F32)
        onesf = T("onesf", [1, 128], F32)
        ident_sb = T("ident_sb", [128, 128], BF16)
        vg1 = T("vg1", [128, 8], F32)
        vb1 = T("vb1", [128, 8], F32)
        vg2 = T("vg2", [128, 8], F32)
        vb2 = T("vb2", [128, 8], F32)
        vg3 = T("vg3", [128, 8], F32)
        vb3 = T("vb3", [128, 8], F32)
        vmk = T("vmk", [128, 8], F32)
        vmv = T("vmv", [128, 8], F32)
        vmr = T("vmr", [128, 8], F32)
        vtf = T("vtf", [128, 8], F32)
        vtd = T("vtd", [128, 8], F32)
        vcx = T("vcx", [128, 8], F32)
        vcy = T("vcy", [128, 8], F32)
        vca = T("vca", [128, 8], F32)
        vcb = T("vcb", [128, 8], F32)
        vfb1 = T("vfb1", [128, 32], F32)
        stats = {}
        for g in ("A", "B"):
            for nmm in ("sum", "m", "nm", "vs", "rs"):
                stats[g + nmm] = T(f"st{g}_{nmm}", [128, 4], F32)
        ce5 = T("ce5", [128, 1], F32)
        ce8 = T("ce8", [128, 1], F32)
        ce52 = T("ce52", [128, 1], F32)
        ones1 = T("ones1", [1, 128], BF16)
        b2row_b = T("b2row_b", [1, 1024], BF16)
        psA = st.enter_context(nc.psum_tensor("psA", [128, 512], F32))
        psB = st.enter_context(nc.psum_tensor("psB", [128, 512], F32))
        psT0 = st.enter_context(nc.psum_tensor("psT0", [128, 512], BF16))
        psT1 = st.enter_context(nc.psum_tensor("psT1", [128, 512], BF16))
        psO = [st.enter_context(nc.psum_tensor(f"psO{i}", [128, 512], F32))
               for i in range(4)]
        E = S.emit

        for name, dst, src_ in [("vg1", vg1, g1), ("vb1", vb1, b1),
                                ("vg2", vg2, g2), ("vb2", vb2, b2),
                                ("vg3", vg3, g3), ("vb3", vb3, b3),
                                ("vmk", vmk, mk), ("vmv", vmv, mvv),
                                ("vmr", vmr, mr), ("vtf", vtf, tf),
                                ("vtd", vtd, td)]:
            E("q_gpsimd", lambda e, dst=dst, src_=src_: e.dma_start(
                out=dst[:], in_=src_.rearrange("(k p) -> p k", p=128),
                allow_slow_non_contiguous=True),
              writes=[name], tag="ld_" + name)
        E("q_gpsimd", lambda e: e.dma_start(
            out=vfb1[:], in_=fb1.rearrange("(m p) -> p m", p=128),
            allow_slow_non_contiguous=True), writes=["vfb1"])
        E("q_gpsimd", lambda e: e.dma_start(out=ident_sb[:], in_=ident),
          writes=["ident_sb"])
        E("q_scalar", lambda e: e.dma_start(
            out=scbc[0:1, 0:1024], in_=fb2.rearrange("(a n) -> a n", a=1)),
          writes=["scbc0", "scbc1"], tag="ld_b2")
        E("vector", lambda e: e.memset(ce5[:], EPS5), writes=["ce5"])
        E("vector", lambda e: e.memset(ce8[:], EPS8), writes=["ce8"])
        E("vector", lambda e: e.memset(ce52[:], EPS5 * EPS5), writes=["ce52"])
        E("vector", lambda e: e.memset(vcx[:], 0.0), writes=["vcx"])
        E("vector", lambda e: e.memset(vcy[:], 0.0), writes=["vcy"])
        E("vector", lambda e: e.memset(vca[:], 0.0), writes=["vca"])
        E("vector", lambda e: e.memset(vcb[:], 0.0), writes=["vcb"])
        E("vector", lambda e: e.memset(ones1[:], 1.0), writes=["ones1"])
        E("vector", lambda e: e.tensor_scalar_mul(out=vg2[:], in0=vg2[:], scalar1=0.5),
          sreads=["vg2"], writes=["vg2"])
        E("vector", lambda e: e.tensor_scalar_mul(out=vb2[:], in0=vb2[:], scalar1=0.5),
          sreads=["vb2"], writes=["vb2"])
        E("vector", lambda e: e.tensor_scalar_mul(
            out=scbc[0:1, 0:1024], in0=scbc[0:1, 0:1024], scalar1=0.5),
          sreads=["scbc0", "scbc1"], writes=["scbc0", "scbc1"], tag="b2_half")
        E("vector", lambda e: e.tensor_copy(out=b2row_b[:], in_=scbc[0:1, 0:1024]),
          sreads=["scbc0", "scbc1"], writes=["b2row_b"], tag="b2_cast")
        E("scalar", lambda e: e.activation(out=vtd[:], in_=vtd[:], func=AF.Exp),
          sreads=["vtd"], writes=["vtd"])
        E("vector", lambda e: e.tensor_scalar_mul(out=vtd[:], in0=vtd[:], scalar1=-1.0),
          sreads=["vtd"], writes=["vtd"])
        E("vector", lambda e: e.memset(ew_t[:], 0.0),
          writes=["ewt0", "ewt1"], tag="ewt_init")
        E("vector", lambda e: e.memset(onesf[:], 1.0), writes=["onesf"],
          tag="onesf_init")

        su_i = [0]
        cast_jobs = []

        def cast_job(tag, in_fn, outs, wname):
            i = su_i[0]
            su_i[0] += 1
            par = i % 2
            fsl = i % 3
            fo = 1024 * fsl
            o = 1024 * par

            def emit_in():
                E(("q_gpsimd", "q_sync", "q_scalar")[fsl],
                  lambda e: e.dma_start(
                      out=stf[:, fo:fo + 1024], in_=in_fn()),
                  writes=[f"stf{fsl}"], tag=f"{tag}i")

            def emit_rest(ceng):
                if ceng == "scalar":
                    E("scalar", lambda e: e.activation(
                        out=stb[:, o:o + 1024], in_=stf[:, fo:fo + 1024],
                        func=AF.Copy),
                      reads=[f"stf{fsl}"], writes=[f"stb{par}"], tag=f"{tag}c")
                else:
                    E("vector", lambda e: e.tensor_copy(
                        out=stb[:, o:o + 1024], in_=stf[:, fo:fo + 1024]),
                      reads=[f"stf{fsl}"], writes=[f"stb{par}"], tag=f"{tag}c")
                for oi, (out_fn, so, w) in enumerate(outs):
                    E("q_gpsimd",
                      lambda e, out_fn=out_fn, so=so, w=w: e.dma_start(
                          out=out_fn(), in_=stb[:, o + so:o + so + w]),
                      reads=[f"stb{par}"], writes=[f"{wname}@{par}"],
                      tag=f"{tag}o{oi}")

            cast_jobs.append((emit_in, emit_rest))

        for wname, whbm, scr in [("s_wk", Wk, s_wk), ("s_wv", Wv, s_wv),
                                 ("s_wr", Wr, s_wr), ("s_wo", Wo, s_wo),
                                 ("s_wfr", Wfr, s_wfr)]:
            for k in range(KT):
                cast_job(f"su_{wname}{k}",
                         lambda k=k, whbm=whbm: whbm[128 * k:128 * (k + 1), :],
                         [(lambda k=k, scr=scr: scr[
                             k // 4, :, 1024 * (k % 4):1024 * (k % 4 + 1)],
                           0, 1024)],
                         wname)
        for m in range(FT):
            cast_job(f"su_fk{m}",
                     lambda m=m: Wfk[:, 128 * m:128 * (m + 1)].rearrange(
                         "(k p) m -> p k m", p=128),
                     [(lambda m=m: s_fk[m // 2, :,
                                        1024 * (m % 2):1024 * (m % 2 + 1)],
                       0, 1024)],
                     "s_fk")
        for m in range(FT):
            cast_job(f"su_fv{m}",
                     lambda m=m: Wfv[128 * m:128 * (m + 1), :],
                     [(lambda m=m, c=c: s_fv[c, m // 2, :,
                                             512 * (m % 2):512 * (m % 2) + 512],
                       512 * c, 512) for c in range(2)],
                     "s_fv")

        def pump_pair(ea="scalar", eb="vector"):
            if not cast_jobs:
                return
            a = cast_jobs.pop(0)
            b = cast_jobs.pop(0) if cast_jobs else None
            a[0]()
            if b is not None:
                b[0]()
            a[1](ea)
            if b is not None:
                b[1](eb)

        quant_fillers = []

        def make_quant(wname, qh, scr, is_w1, m):
            qi_par = m % 2
            o2 = 1024 * qi_par
            srb = f"srw{qi_par}"

            def emit_loads():
                E("q_scalar", lambda e: e.dma_start(
                    out=stf[0:1, o2:o2 + 1024].rearrange(
                        "a (k m) -> a k m", k=8) if is_w1 else
                    stf[0:1, o2:o2 + 1024],
                    in_=s1[0:8, 128 * m:128 * (m + 1)].partition_broadcast(1)
                    if is_w1 else qh_s[m:m + 1, :]),
                  writes=[srb, f"stf{qi_par}"], tag=f"su_{wname}{m}sr")
                for c in range(2):
                    sl = (2 * m + c) % 4
                    ss = slice(512 * sl, 512 * (sl + 1))
                    qb = f"stq{sl}"
                    if is_w1:
                        E("q_scalar", lambda e, c=c, ss=ss: e.dma_start(
                            out=stq[:, ss].rearrange("p (k m) -> p k m", k=4),
                            in_=qh[512 * c:512 * (c + 1),
                                   128 * m:128 * (m + 1)].rearrange(
                                "(k p) m -> p k m", p=128)),
                          writes=[qb], tag=f"su_{wname}{m}q{c}")
                    else:
                        E("q_scalar", lambda e, c=c, ss=ss: e.dma_start(
                            out=stq[:, ss],
                            in_=qh[128 * m:128 * (m + 1), 512 * c:512 * (c + 1)]),
                          writes=[qb], tag=f"su_{wname}{m}q{c}")

            def emit_rest():
                par = m % 2
                o = 1024 * par
                for c, (psx, psxn) in enumerate([(psA, "psA"), (psB, "psB")]):
                    E("tensor", lambda e, c=c, psx=psx: e.matmul(
                        psx[:], onesf[:],
                        stf[0:1, o2 + 512 * c:o2 + 512 * (c + 1)],
                        start=True, stop=True),
                      reads=[srb, "onesf"], writes=[psxn],
                      tag=f"su_{wname}{m}bc{c}")
                    E("scalar", lambda e, c=c, psx=psx: e.activation(
                        out=scbc[:, 512 * c:512 * (c + 1)], in_=psx[:],
                        func=AF.Copy),
                      reads=[psxn], writes=[f"scbc{c}"],
                      tag=f"su_{wname}{m}se{c}")
                for c in range(2):
                    sl = (2 * m + c) % 4
                    ss = slice(512 * sl, 512 * (sl + 1))
                    cs = slice(512 * c, 512 * (c + 1))
                    qb, sb = f"stq{sl}", f"scbc{c}"
                    if is_w1:
                        E("vector", lambda e, o=o, c=c, ss=ss, cs=cs: e.scalar_tensor_tensor(
                            out=stb[:, o + 512 * c:o + 512 * (c + 1)],
                            in0=stq[:, ss], scalar=-128.0,
                            in1=scbc[:, cs], op0=OP.add, op1=OP.mult),
                          reads=[qb, sb], writes=[f"stb{par}"],
                          tag=f"su_{wname}{m}d{c}")
                    else:
                        E("vector", lambda e, ss=ss: e.tensor_scalar(
                            out=zt2[:, ss], in0=stq[:, ss], scalar1=-128.0,
                            scalar2=0.5, op0=OP.add, op1=OP.mult),
                          reads=[qb], writes=[f"zt2_{sl // 2}"],
                          tag=f"su_{wname}{m}dh{c}")
                        E("vector", lambda e, o=o, c=c, ss=ss, cs=cs: e.tensor_tensor(
                            out=stb[:, o + 512 * c:o + 512 * (c + 1)],
                            in0=zt2[:, ss], in1=scbc[:, cs], op=OP.mult),
                          reads=[f"zt2_{sl // 2}", sb], writes=[f"stb{par}"],
                          tag=f"su_{wname}{m}d{c}")
                if is_w1:
                    E("q_gpsimd", lambda e, o=o: e.dma_start(
                        out=scr[m // 2, :, 1024 * (m % 2):1024 * (m % 2 + 1)],
                        in_=stb[:, o:o + 1024]),
                      reads=[f"stb{par}"], writes=[f"{wname}@{par}"],
                      tag=f"su_{wname}{m}o")
                else:
                    for c in range(2):
                        E("q_gpsimd", lambda e, o=o, c=c: e.dma_start(
                            out=scr[c, m // 2, :, 512 * (m % 2):512 * (m % 2) + 512],
                            in_=stb[:, o + 512 * c:o + 512 * (c + 1)]),
                          reads=[f"stb{par}"], writes=[f"{wname}@{par}"],
                          tag=f"su_{wname}{m}o{c}")
            qh_s = s2 if not is_w1 else None
            return emit_loads, emit_rest

        for m in range(FT):
            quant_fillers.append(make_quant("s_w1", q1, s_w1, True, m))
        for m in range(FT):
            quant_fillers.append(make_quant("s_w2", q2, s_w2, False, m))


        def bbuf(slot):
            return f"bg{slot}"

        def ln_tile(tag, j, srcten, srcbuf, sg, dumpten, dumpbuf):
            ssum, sm, snm, svs = (stats[sg + "sum"], stats[sg + "m"],
                                  stats[sg + "nm"], stats[sg + "vs"])
            bsum, bm, bnm, bvs = (f"st{sg}sum", f"st{sg}m", f"st{sg}nm",
                                  f"st{sg}vs")
            E("vector", lambda e, j=j: e.reduce_sum(
                out=ssum[:, j:j + 1], in_=srcten[:, 1024 * j:1024 * (j + 1)],
                axis=AX.X), reads=[srcbuf], writes=[bsum], tag=f"{tag}_s{j}")
            E("vector", lambda e, j=j: e.tensor_scalar_mul(
                out=snm[:, j:j + 1], in0=ssum[:, j:j + 1], scalar1=-1.0 / D),
              sreads=[bsum], writes=[bnm], tag=f"{tag}_nm{j}")
            E("vector", lambda e, j=j: e.tensor_scalar_mul(
                out=sm[:, j:j + 1], in0=ssum[:, j:j + 1], scalar1=1.0 / D),
              sreads=[bsum], writes=[bm], tag=f"{tag}_m{j}")
            E("scalar", lambda e, j=j: e.activation(
                out=dumpten[:, 1024 * j:1024 * (j + 1)],
                in_=srcten[:, 1024 * j:1024 * (j + 1)], func=AF.Square,
                bias=snm[:, j:j + 1], accum_out=svs[:, j:j + 1]),
              reads=[srcbuf], sreads=[bnm],
              writes=[f"{dumpbuf}_{j}" if dumpbuf in ("ztm", "zt2")
                      else dumpbuf, bvs],
              tag=f"{tag}_q{j}")

        def ln_fin(tag, srcten, srcbuf, dstten, dstbuf, sg, post_cb=None):
            sm, svs, srs = stats[sg + "m"], stats[sg + "vs"], stats[sg + "rs"]
            bm, bvs, brs = f"st{sg}m", f"st{sg}vs", f"st{sg}rs"
            E("scalar", lambda e: e.activation(
                out=srs[:], in_=svs[:], func=AF.Ln, scale=1.0 / D, bias=ce5[:]),
              sreads=[bvs, "ce5"], writes=[brs], tag=f"{tag}_l")
            E("scalar", lambda e: e.activation(
                out=srs[:], in_=srs[:], func=AF.Exp, scale=-0.5),
              sreads=[brs], writes=[brs], tag=f"{tag}_e")
            for j in range(TT):
                E("vector", lambda e, j=j: e.tensor_scalar(
                    out=dstten[:, 1024 * j:1024 * (j + 1)],
                    in0=srcten[:, 1024 * j:1024 * (j + 1)],
                    scalar1=sm[:, j:j + 1], scalar2=srs[:, j:j + 1],
                    op0=OP.subtract, op1=OP.mult),
                  reads=[srcbuf], sreads=[bm, brs],
                  writes=[f"{dstbuf}_{j}" if dstbuf in ("ztm", "zt2")
                          else dstbuf],
                  tag=f"{tag}_z{j}")
                if post_cb is not None:
                    post_cb(j)

        def emit_ln(tag, srcten, srcbuf, dstten, dstbuf, sg, dumpten, dumpbuf,
                    post_cb=None):
            for j in range(TT):
                ln_tile(tag, j, srcten, srcbuf, sg, dumpten, dumpbuf)
            ln_fin(tag, srcten, srcbuf, dstten, dstbuf, sg, post_cb)

        def emit_transpose(tag, srcten, srcbuf, dstten, dstbuf, gv, bv, gname, bname):
            for ci in range(KT):
                pst = psT0 if ci % 2 == 0 else psT1
                pbn = f"psT{ci % 2}"
                for j in range(TT):
                    E("tensor", lambda e, ci=ci, j=j, pst=pst: e.transpose(
                        pst[:, 128 * j:128 * (j + 1)],
                        srcten[:, 1024 * j + 128 * ci:1024 * j + 128 * (ci + 1)],
                        ident_sb[:]),
                      reads=[f"{srcbuf}_{j}", "ident_sb"], writes=[pbn],
                      tag=f"{tag}_t{ci}_{j}")
                E("scalar", lambda e, ci=ci, pst=pst: e.activation(
                    out=dstten[:, 512 * ci:512 * (ci + 1)], in_=pst[:],
                    func=AF.Identity, scale=gv[:, ci:ci + 1], bias=bv[:, ci:ci + 1]),
                  reads=[pbn], sreads=[gname, bname], writes=[f"{dstbuf}{ci}"],
                  tag=f"{tag}_e{ci}")

        def stage_pre(h):
            hh = f"h{h}"
            E("q_sync", lambda e, h=h: e.dma_start(
                out=x0[:].rearrange("p (j n) -> p j n", j=4),
                in_=xcore[HALF * h:HALF * (h + 1)].rearrange(
                    "(j p) n -> p j n", p=128)),
              writes=["x0"], tag=f"{hh}_x0")
            emit_ln(f"{hh}_ln1", x0, "x0", ztm, "ztm", "A", ztm, "ztm")

        def stage_tr1(h):
            emit_transpose(f"h{h}_tr1", ztm, "ztm", TA, "TA", vg1, vb1,
                           "vg1", "vb1")

        def mix_one(h, vm, vmn, slot, upd_carry):
            hh = f"h{h}"
            for ci in range(KT):
                E("vector", lambda e, ci=ci: e.tensor_tensor(
                    out=sc_bb[:, 1:512], in0=TA[:, 512 * ci + 1:512 * (ci + 1)],
                    in1=TA[:, 512 * ci:512 * (ci + 1) - 1], op=OP.subtract),
                  reads=[f"TA{ci}"], writes=["bb0"], tag=f"{hh}_d{slot}_{ci}")
                E("vector", lambda e, ci=ci: e.tensor_tensor(
                    out=sc_bb[:, 0:1], in0=TA[:, 512 * ci:512 * ci + 1],
                    in1=vcx[:, ci:ci + 1], op=OP.subtract),
                  reads=[f"TA{ci}"], sreads=["vcx"], writes=["bb0"],
                  tag=f"{hh}_d0{slot}_{ci}")
                E("vector", lambda e, ci=ci, vm=vm: e.scalar_tensor_tensor(
                    out=bigbf[:, 512 * (slot + ci) + 1:512 * (slot + ci + 1)],
                    in0=sc_bb[:, 1:512], scalar=vm[:, ci:ci + 1],
                    in1=TA[:, 512 * ci:512 * (ci + 1) - 1],
                    op0=OP.mult, op1=OP.add),
                  reads=["bb0", f"TA{ci}"], sreads=[vmn],
                  writes=[bbuf(slot + ci)], tag=f"{hh}_mx{slot}_{ci}")
                E("vector", lambda e, ci=ci, vm=vm: e.scalar_tensor_tensor(
                    out=bigbf[:, 512 * (slot + ci):512 * (slot + ci) + 1],
                    in0=sc_bb[:, 0:1], scalar=vm[:, ci:ci + 1],
                    in1=vcx[:, ci:ci + 1], op0=OP.mult, op1=OP.add),
                  reads=["bb0"], sreads=[vmn, "vcx"], writes=[bbuf(slot + ci)],
                  tag=f"{hh}_mx0{slot}_{ci}")
                if upd_carry:
                    E("vector", lambda e, ci=ci: e.tensor_copy(
                        out=vcx[:, ci:ci + 1],
                        in_=TA[:, 512 * (ci + 1) - 1:512 * (ci + 1)]),
                      sreads=[f"TA{ci}"], writes=["vcx"], tag=f"{hh}_cx{ci}")

        def stage_mix_k(h):
            mix_one(h, vmk, "vmk", 0, False)

        def stage_mix_vr(h):
            mix_one(h, vmv, "vmv", 8, False)
            mix_one(h, vmr, "vmr", 16, True)

        KVPS = [(psA, "psA"), (psB, "psB"), (psO[0], "psO0"), (psO[1], "psO1")]
        RPS = [(psO[2], "psO2"), (psO[3], "psO3")]

        def kvr_mm(hh, mo, wlo, wlo_n, whi, whi_n, slot, tag, pool=None):
            ps, psn = (pool or KVPS)[mo % len(pool or KVPS)]
            for k in range(KT):
                wt, wn = (wlo, wlo_n) if k < 4 else (whi, whi_n)
                E("tensor", lambda e, mo=mo, k=k, ps=ps, wt=wt: e.matmul(
                    ps[:],
                    wt[:, 1024 * (k % 4) + 128 * mo:1024 * (k % 4) + 128 * (mo + 1)],
                    bigbf[:, 512 * (slot + k):512 * (slot + k + 1)],
                    start=(k == 0), stop=(k == KT - 1)),
                  reads=[wn, bbuf(slot + k)], writes=[psn], tag=f"{tag}{mo}_{k}")
            return ps, psn

        def stage_attn(h):
            hh = f"h{h}"
            E("q_sync", lambda e: e.dma_start(out=wsm[0][:], in_=s_wk[0]),
              reads=["s_wk@0", "s_wk@1"], writes=["wsm0"], tag=f"{hh}_ldk0")
            E("q_sync", lambda e: e.dma_start(out=wsm[1][:], in_=s_wk[1]),
              reads=["s_wk@0", "s_wk@1"], writes=["wsm1"], tag=f"{hh}_ldk1")
            E("q_sync", lambda e: e.dma_start(out=wsm[2][:], in_=s_wv[0]),
              reads=["s_wv@0", "s_wv@1"], writes=["wsm2"], tag=f"{hh}_ldv0")
            for mo in range(KT):
                ps, psn = kvr_mm(hh, mo, wsm[0], "wsm0", wsm[1], "wsm1", 0,
                                 f"{hh}_kmm")
                E("scalar", lambda e, mo=mo, ps=ps: e.activation(
                    out=T1[:, 512 * mo:512 * (mo + 1)], in_=ps[:], func=AF.Exp),
                  reads=[psn], writes=["T1"], tag=f"{hh}_ek{mo}")
                if h == 0:
                    E("scalar", lambda e, mo=mo, ps=ps: e.activation(
                        out=T1[:, 512 * mo:512 * mo + 1], in_=ps[:, 0:1],
                        func=AF.Exp, bias=vtf[:, mo:mo + 1]),
                      reads=[psn], sreads=["vtf"], writes=["T1"],
                      tag=f"{hh}_ektf{mo}")
                    pump_pair()
                    pump_pair()
            E("q_sync", lambda e: e.dma_start(out=wsm[0][:], in_=s_wv[1]),
              reads=["s_wv@0", "s_wv@1"], writes=["wsm0"], tag=f"{hh}_ldv1")
            E("q_sync", lambda e: e.dma_start(out=wsm[1][:], in_=s_wr[0]),
              reads=["s_wr@0", "s_wr@1"], writes=["wsm1"], tag=f"{hh}_ldr0")
            for mo in range(KT):
                ps, psn = kvr_mm(hh, mo, wsm[2], "wsm2", wsm[0], "wsm0", 8,
                                 f"{hh}_vmm")
                E("vector", lambda e, mo=mo, ps=ps: e.tensor_tensor(
                    out=sc_ekv[:], in0=T1[:, 512 * mo:512 * (mo + 1)], in1=ps[:],
                    op=OP.mult),
                  reads=["T1", psn], writes=["ekv"], tag=f"{hh}_ekv{mo}")
                sl_e = mo % 2
                ese = slice(512 * sl_e, 512 * (sl_e + 1))
                E("scalar", lambda e, mo=mo, ese=ese: e.activation(
                    out=ew_t[:, ese], in_=ew_t[:, ese], func=AF.Exp, scale=0.0,
                    bias=vtd[:, mo:mo + 1]),
                  reads=[f"ewt{sl_e}"], sreads=["vtd"], writes=[f"ewt{sl_e}"],
                  tag=f"{hh}_ew{mo}")
                E("vector", lambda e, mo=mo, ese=ese: e.tensor_tensor_scan(
                    out=sc_aa[:], data0=ew_t[:, ese],
                    data1=sc_ekv[:], initial=vca[:, mo:mo + 1],
                    op0=OP.mult, op1=OP.add),
                  reads=[f"ewt{sl_e}", "ekv"], sreads=["vca"], writes=["aa"],
                  tag=f"{hh}_sa{mo}")
                E("vector", lambda e, mo=mo: e.tensor_copy(
                    out=vca[:, mo:mo + 1], in_=sc_aa[:, 511:512]),
                  sreads=["aa"], writes=["vca"], tag=f"{hh}_ca{mo}")
                bo = 512 * (mo % 2)
                bn = f"bb{mo % 2}"
                E("vector", lambda e, mo=mo, ese=ese, bo=bo: e.tensor_tensor_scan(
                    out=sc_bb[:, bo:bo + 512], data0=ew_t[:, ese],
                    data1=T1[:, 512 * mo:512 * (mo + 1)],
                    initial=vcb[:, mo:mo + 1], op0=OP.mult, op1=OP.add),
                  reads=[f"ewt{sl_e}", "T1"], sreads=["vcb"], writes=[bn],
                  tag=f"{hh}_sb{mo}")
                E("vector", lambda e, mo=mo, bo=bo: e.tensor_copy(
                    out=vcb[:, mo:mo + 1], in_=sc_bb[:, bo + 511:bo + 512]),
                  sreads=[bn], writes=["vcb"], tag=f"{hh}_cb{mo}")
                E("scalar", lambda e, bo=bo: e.activation(
                    out=sc_bb[:, bo:bo + 512], in_=sc_bb[:, bo:bo + 512],
                    func=AF.Ln, bias=ce8[:]),
                  sreads=[bn, "ce8"], writes=[bn], tag=f"{hh}_lb{mo}")
                E("scalar", lambda e, bo=bo: e.activation(
                    out=sc_bb[:, bo:bo + 512], in_=sc_bb[:, bo:bo + 512],
                    func=AF.Exp, scale=-1.0),
                  sreads=[bn], writes=[bn], tag=f"{hh}_eb{mo}")
                E("vector", lambda e, mo=mo, bo=bo: e.tensor_tensor(
                    out=T2[:, 512 * mo:512 * (mo + 1)], in0=sc_aa[:],
                    in1=sc_bb[:, bo:bo + 512], op=OP.mult),
                  reads=["aa", bn], writes=["T2"], tag=f"{hh}_wk{mo}")
                if h == 0:
                    pump_pair("scalar", "scalar")
            E("q_sync", lambda e: e.dma_start(out=wsm[2][:], in_=s_wr[1]),
              reads=["s_wr@0", "s_wr@1"], writes=["wsm2"], tag=f"{hh}_ldr1")
            E("q_sync", lambda e: e.dma_start(out=wsm[0][:], in_=s_wo[0]),
              reads=["s_wo@0", "s_wo@1"], writes=["wsm0"], tag=f"{hh}_ldo0")
            sig_fns = []
            for mo in range(KT):
                ps, psn = kvr_mm(hh, mo, wsm[1], "wsm1", wsm[2], "wsm2", 16,
                                 f"{hh}_rmm", pool=RPS)

                def mkf(mo=mo, ps=ps, psn=psn):
                    E("scalar", lambda e: e.activation(
                        out=rsg[:], in_=ps[:], func=AF.Sigmoid),
                      reads=[psn], writes=["rsg"], tag=f"{hh}_sg{mo}")
                    E("vector", lambda e: e.tensor_tensor(
                        out=bigbf[:, 512 * (24 + mo):512 * (25 + mo)],
                        in0=rsg[:], in1=T2[:, 512 * mo:512 * (mo + 1)],
                        op=OP.mult),
                      reads=["rsg", "T2"], writes=[bbuf(24 + mo)],
                      tag=f"{hh}_gt{mo}")
                sig_fns.append(mkf)
                if mo > 0:
                    sig_fns[mo - 1]()
                if h == 0:
                    pump_pair()
                    pump_pair()
            sig_fns[KT - 1]()
            E("q_sync", lambda e: e.dma_start(out=wsm[1][:], in_=s_wo[1]),
              reads=["s_wo@0", "s_wo@1"], writes=["wsm1"], tag=f"{hh}_ldo1")
            E("q_sync", lambda e: e.dma_start(out=wsm[2][:], in_=s_wfr[0]),
              reads=["s_wfr@0", "s_wfr@1"], writes=["wsm2"], tag=f"{hh}_ldf0")
            for j in range(TT):
                p0, p1 = psO[2 * (j % 2)], psO[2 * (j % 2) + 1]
                p0n, p1n = f"psO{2 * (j % 2)}", f"psO{2 * (j % 2) + 1}"
                for k in range(KT):
                    wt, wn = (wsm[0], "wsm0") if k < 4 else (wsm[1], "wsm1")
                    gsl = slice(512 * (24 + k) + 128 * j,
                                512 * (24 + k) + 128 * (j + 1))
                    E("tensor", lambda e, p0=p0, wt=wt, k=k, gsl=gsl: e.matmul(
                        p0[:], bigbf[:, gsl],
                        wt[:, 1024 * (k % 4):1024 * (k % 4) + 512],
                        start=(k == 0), stop=(k == KT - 1)),
                      reads=[bbuf(24 + k), wn], writes=[p0n],
                      tag=f"{hh}_wo{j}_{k}a")
                    E("tensor", lambda e, p1=p1, wt=wt, k=k, gsl=gsl: e.matmul(
                        p1[:], bigbf[:, gsl],
                        wt[:, 1024 * (k % 4) + 512:1024 * (k % 4) + 1024],
                        start=(k == 0), stop=(k == KT - 1)),
                      reads=[bbuf(24 + k), wn], writes=[p1n],
                      tag=f"{hh}_wo{j}_{k}b")
                for hi, pp, ppn in [(0, p0, p0n), (1, p1, p1n)]:
                    off = 1024 * j + 512 * hi
                    E("vector", lambda e, pp=pp, off=off: e.tensor_tensor(
                        out=Q[:, off:off + 512], in0=x0[:, off:off + 512],
                        in1=pp[:], op=OP.add),
                      reads=["x0", ppn], writes=["Q"], tag=f"{hh}_u{j}{hi}")
                    E("vector", lambda e, pp=pp, off=off: e.scalar_tensor_tensor(
                        out=T2[:, off:off + 512], in0=x0[:, off:off + 512],
                        scalar=2.0, in1=pp[:], op0=OP.mult, op1=OP.add),
                      reads=["x0", ppn], writes=["T2"], tag=f"{hh}_s01{j}{hi}")
                ln_tile(f"{hh}_ln2", j, Q, "Q", "B", ztm, "ztm")
                if h == 0:
                    pump_pair()
            E("q_sync", lambda e: e.dma_start(out=wsm[0][:], in_=s_wfr[1]),
              reads=["s_wfr@0", "s_wfr@1"], writes=["wsm0"], tag=f"{hh}_ldf1")
            ln_fin(f"{hh}_ln2", Q, "Q", ztm, "ztm", "B")
            emit_transpose(f"{hh}_tr2", ztm, "ztm", TA, "TA", vg2, vb2,
                           "vg2", "vb2")
            for ci in range(KT):
                E("vector", lambda e, ci=ci: e.tensor_tensor(
                    out=TB[:, 512 * ci + 1:512 * (ci + 1)],
                    in0=TA[:, 512 * ci + 1:512 * (ci + 1)],
                    in1=TA[:, 512 * ci:512 * (ci + 1) - 1], op=OP.add),
                  reads=[f"TA{ci}"], writes=[f"TB{ci}"], tag=f"{hh}_ym{ci}")
                E("vector", lambda e, ci=ci: e.tensor_tensor(
                    out=TB[:, 512 * ci:512 * ci + 1],
                    in0=TA[:, 512 * ci:512 * ci + 1], in1=vcy[:, ci:ci + 1],
                    op=OP.add),
                  reads=[f"TA{ci}"], sreads=["vcy"], writes=[f"TB{ci}"],
                  tag=f"{hh}_ym0{ci}")
                E("vector", lambda e, ci=ci: e.tensor_copy(
                    out=vcy[:, ci:ci + 1],
                    in_=TA[:, 512 * (ci + 1) - 1:512 * (ci + 1)]),
                  sreads=[f"TA{ci}"], writes=["vcy"], tag=f"{hh}_cy{ci}")
            for j in range(TT):
                p0, p1 = psO[2 * (j % 2)], psO[2 * (j % 2) + 1]
                p0n, p1n = f"psO{2 * (j % 2)}", f"psO{2 * (j % 2) + 1}"
                for k in range(KT):
                    wt, wn = (wsm[2], "wsm2") if k < 4 else (wsm[0], "wsm0")
                    asl = slice(512 * k + 128 * j, 512 * k + 128 * (j + 1))
                    E("tensor", lambda e, p0=p0, wt=wt, k=k, asl=asl: e.matmul(
                        p0[:], TA[:, asl],
                        wt[:, 1024 * (k % 4):1024 * (k % 4) + 512],
                        start=(k == 0), stop=(k == KT - 1)),
                      reads=[f"TA{k}", wn], writes=[p0n],
                      tag=f"{hh}_fr{j}_{k}a")
                    E("tensor", lambda e, p1=p1, wt=wt, k=k, asl=asl: e.matmul(
                        p1[:], TA[:, asl],
                        wt[:, 1024 * (k % 4) + 512:1024 * (k % 4) + 1024],
                        start=(k == 0), stop=(k == KT - 1)),
                      reads=[f"TA{k}", wn], writes=[p1n],
                      tag=f"{hh}_fr{j}_{k}b")
                for hi, pp, ppn in [(0, p0, p0n), (1, p1, p1n)]:
                    off = 1024 * j + 512 * hi
                    E("scalar", lambda e, pp=pp, off=off: e.activation(
                        out=T1[:, off:off + 512], in_=pp[:], func=AF.Sigmoid,
                        scale=2.0),
                      reads=[ppn], writes=["T1"], tag=f"{hh}_rr{j}{hi}")

        def emit_mm2(tagp, mp, cho, bias_close):
            mm, mpar = mp
            sl = (mm // 2) % 2
            last = (mm == FT - 1)
            for j in range(TT):
                E("tensor", lambda e, mm=mm, j=j, mpar=mpar, sl=sl: e.matmul(
                    psO[j][:],
                    bigbf[:, 512 * mm + 128 * j:512 * mm + 128 * (j + 1)],
                    fvb[:, 1024 * sl + 512 * mpar:1024 * sl + 512 * (mpar + 1)],
                    start=(mm == 0), stop=(last and not bias_close)),
                  reads=[bbuf(mm), f"fvb{2 * sl + mpar}"], writes=[f"psO{j}"],
                  tag=f"{tagp}m2_{cho}_{mm}_{j}")

        def ffn_pair(tagp, blkn, rown, scr_blk, scr_row, evac_fn, comb_fn,
                     bias_close, fillers=None):
            for cho in range(2):
                pend = []
                for m in range(FT):
                    par = m % 2
                    sl = (m // 2) % 2
                    if cho == 0:
                        E("q_sync", lambda e, m=m, par=par: e.dma_start(
                            out=fkb[:, 1024 * par:1024 * (par + 1)],
                            in_=scr_blk[m // 2, :,
                                        1024 * (m % 2):1024 * (m % 2 + 1)]),
                          reads=[blkn + "@0", blkn + "@1"],
                          writes=[f"fkb{par}"], tag=f"{tagp}b{m}")
                    if par == 0:
                        E("q_sync", lambda e, m=m, cho=cho, sl=sl: e.dma_start(
                            out=fvb[:, 1024 * sl:1024 * (sl + 1)],
                            in_=scr_row[cho, m // 2]),
                          reads=[rown + "@0", rown + "@1"],
                          writes=[f"fvb{2 * sl}", f"fvb{2 * sl + 1}"],
                          tag=f"{tagp}r{m}_{cho}")
                    if cho == 0:
                        ps, psn = (psA, "psA") if par == 0 else (psB, "psB")
                        for k in range(KT):
                            E("tensor", lambda e, m=m, k=k, ps=ps, par=par: e.matmul(
                                ps[:],
                                fkb[:, 1024 * par + 128 * k:1024 * par + 128 * (k + 1)],
                                TB[:, 512 * k:512 * (k + 1)],
                                start=(k == 0), stop=(k == KT - 1)),
                              reads=[f"fkb{par}", f"TB{k}"], writes=[psn],
                              tag=f"{tagp}m1_{m}_{k}")
                        evac_fn(m, ps, psn, par)
                    if fillers:
                        for f in fillers(m, cho):
                            f()
                    pend.append((m, par))
                    if len(pend) >= 2:
                        emit_mm2(tagp, pend.pop(0), cho, bias_close)
                while pend:
                    emit_mm2(tagp, pend.pop(0), cho, bias_close)
                comb_fn(cho)

        def stage_fA(h, fillers=None):
            hh = f"h{h}"

            def kk_evac(m, ps, psn, par):
                E("scalar", lambda e, ps=ps, m=m: e.activation(
                    out=bigbf[:, 512 * m:512 * (m + 1)], in_=ps[:],
                    func=AF.Relu),
                  reads=[psn], writes=[bbuf(m)], tag=f"{hh}_re{m}")
                E("vector", lambda e, m=m: e.tensor_tensor(
                    out=bigbf[:, 512 * m:512 * (m + 1)],
                    in0=bigbf[:, 512 * m:512 * (m + 1)],
                    in1=bigbf[:, 512 * m:512 * (m + 1)], op=OP.mult),
                  reads=[bbuf(m)], writes=[bbuf(m)], tag=f"{hh}_sq{m}")

            def comb1(cho):
                for j in range(TT):
                    off = 1024 * j + 512 * cho
                    E("vector", lambda e, j=j, off=off: e.tensor_tensor(
                        out=sc_ekv[:], in0=T1[:, off:off + 512], in1=psO[j][:],
                        op=OP.mult),
                      reads=["T1", f"psO{j}"], writes=["ekv"],
                      tag=f"{hh}_g1{cho}{j}")
                    E("vector", lambda e, off=off: e.tensor_tensor(
                        out=T2[:, off:off + 512], in0=T2[:, off:off + 512],
                        in1=sc_ekv[:], op=OP.add),
                      reads=["ekv", "T2"], writes=["T2"], tag=f"{hh}_a1{cho}{j}")
                    if cho == 1:
                        ln_tile(f"{hh}_pl1", j, T2, "T2", "B", zt2, "zt2")

            ffn_pair(f"{hh}_fA", "s_fk", "s_fv", s_fk, s_fv, kk_evac, comb1,
                     False, fillers)

        def stage_mid(h):
            hh = f"h{h}"
            svs = stats["Bvs"]
            sS = stats["Ars"]
            sm = stats["Bm"]
            E("scalar", lambda e: e.activation(
                out=sS[:], in_=svs[:], func=AF.Ln, scale=(1.0 + EPS5) / D,
                bias=ce52[:]),
              sreads=["stBvs", "ce52"], writes=["stArs"], tag=f"{hh}_r5")
            E("scalar", lambda e: e.activation(
                out=sS[:], in_=sS[:], func=AF.Exp, scale=-0.5),
              sreads=["stArs"], writes=["stArs"], tag=f"{hh}_r6")
            for j in range(TT):
                E("vector", lambda e, j=j: e.tensor_scalar(
                    out=zt2[:, 1024 * j:1024 * (j + 1)],
                    in0=T2[:, 1024 * j:1024 * (j + 1)],
                    scalar1=sm[:, j:j + 1], scalar2=sS[:, j:j + 1],
                    op0=OP.subtract, op1=OP.mult),
                  reads=["T2"], sreads=["stBm", "stArs"],
                  writes=[f"zt2_{j}"], tag=f"{hh}_z3{j}")
            emit_transpose(f"{hh}_tr3", zt2, "zt2", TB, "TB", vg3, vb3,
                           "vg3", "vb3")
            ln_fin(f"{hh}_pl1", T2, "T2", Q, "Q", "B")

        def stage_fB(h):
            hh = f"h{h}"

            def silu_evac(m, ps, psn, par):
                E("scalar", lambda e, m=m, ps=ps: e.activation(
                    out=bigbf[:, 512 * m:512 * (m + 1)], in_=ps[:], func=AF.Silu,
                    bias=vfb1[:, m:m + 1]),
                  reads=[psn], sreads=["vfb1"], writes=[bbuf(m)],
                  tag=f"{hh}_si{m}")

            def comb2(cho):
                for j in range(TT):
                    E("tensor", lambda e, j=j, cho=cho: e.matmul(
                        psO[j][:], ones1[:],
                        b2row_b[0:1, 512 * cho:512 * (cho + 1)],
                        start=False, stop=True),
                      reads=["ones1", "b2row_b"], writes=[f"psO{j}"],
                      tag=f"{hh}_bz{cho}{j}")
                    off = 1024 * j + 512 * cho
                    E("vector", lambda e, j=j, off=off: e.scalar_tensor_tensor(
                        out=T2[:, off:off + 512], in0=Q[:, off:off + 512],
                        scalar=2.0, in1=psO[j][:], op0=OP.mult, op1=OP.add),
                      reads=["Q", f"psO{j}"], writes=["T2"],
                      tag=f"{hh}_c2{cho}{j}")

            ffn_pair(f"{hh}_fB", "s_w1", "s_w2", s_w1, s_w2, silu_evac, comb2,
                     True, fillers=mk_fB_fillers(h))

        def stage_tail(h):
            hh = f"h{h}"
            for j in range(TT):
                ln_tile(f"{hh}_pl2", j, T2, "T2", "B", zt2, "zt2")
            ln_fin(f"{hh}_pl2", T2, "T2", Q, "Q", "B")
            E("q_gpsimd", lambda e, h=h: e.dma_start(
                out=out_h[HALF * h:HALF * (h + 1)].rearrange(
                    "(j p) n -> p j n", p=128),
                in_=Q[:].rearrange("p (j n) -> p j n", j=4)),
              reads=["Q"], writes=[f"od{h}"], tag=f"{hh}_out")

        def mk_fillers(h):
            def fillers(m, cho):
                out = []
                if cho != 0:
                    return out
                if h == 0:
                    for qi in (2 * m, 2 * m + 1):
                        if qi < len(quant_fillers):
                            out.append(quant_fillers[qi][1])
                    for qi in (2 * m + 2, 2 * m + 3):
                        if qi < len(quant_fillers):
                            out.append(quant_fillers[qi][0])
                if m == 4 and h + 1 < NH:
                    out.append(lambda: stage_pre(h + 1))
                return out
            return fillers

        def mk_fB_fillers(h):
            def fillers(m, cho):
                if cho == 1 and m == 10 and h + 1 < NH:
                    return [lambda: stage_mix_k(h + 1)]
                return []
            return fillers

        pump_pair()
        pump_pair()
        stage_pre(0)
        pump_pair()
        pump_pair()
        stage_tr1(0)
        pump_pair()
        pump_pair()
        stage_mix_k(0)
        pump_pair()
        pump_pair()
        for h in range(NH):
            stage_mix_vr(h)
            stage_attn(h)
            while cast_jobs:
                pump_pair()
            if h == 0:
                quant_fillers[0][0]()
                quant_fillers[1][0]()
            stage_fA(h, fillers=mk_fillers(h))
            if h + 1 < NH:
                stage_tr1(h + 1)
            stage_mid(h)
            stage_fB(h)
            stage_tail(h)

        E("gpsimd", lambda e: None, reads=[f"od{h}" for h in range(NH)],
          tag="final_wait")
        S.build()
        sems = {}
        for key in sorted({it["skey"] for it in S.items}):
            sems[key] = st.enter_context(nc.semaphore("sem_" + key))
        S.sems = sems
        block = st.enter_context(nc.Block())
        S.materialize(block)
    return nc


_NC_CACHE = None


def kernel(**inputs):
    global _NC_CACHE
    x = np.ascontiguousarray(inputs["L0_attn_x"])
    if _NC_CACHE is None:
        _NC_CACHE = build_program()
    nc = _NC_CACHE
    ident = np.eye(128, dtype=ml_dtypes.bfloat16)
    base = {k: np.ascontiguousarray(v) for k, v in inputs.items()
            if k != "L0_attn_x"}
    in_maps = []
    for i in range(8):
        m = dict(base)
        m["xcore"] = np.ascontiguousarray(x[i])
        m["ident"] = ident
        in_maps.append(m)
    res = run_bass_kernel_spmd(nc, in_maps, core_ids=list(range(8)))
    out = np.stack([res.results[i]["out"] for i in range(8)], axis=0)
    return out.astype(np.float32)
